# version 35
# baseline (speedup 1.0000x reference)
import math
import contextlib
import numpy as np
import ml_dtypes
import concourse.bass as bass
import concourse.mybir as mybir
from concourse.bass_utils import run_bass_kernel_spmd

F32 = mybir.dt.float32
BF16 = mybir.dt.bfloat16
AF = mybir.ActivationFunctionType
ALU = mybir.AluOpType
AX = mybir.AxisListType
bfd = ml_dtypes.bfloat16

NCORES = 8
NSEQ = 2
L = 2048
D = 1024
NTC = 16
EPS = 1e-5
CEXP = 4.0
SM_SCALE = 1.0 / 32.0

LIVE = [(h, e) for e in range(4) for h in range(8 >> e)]
BLK = {he: i for i, he in enumerate(LIVE)}
PACKW = len(LIVE) * 512
PFX_E = {1: 3, 2: 2, 4: 1, 8: 0}


def _gammas():
    lin = np.linspace(np.float32(math.log(1.0 / 32.0)),
                      np.float32(math.log(1.0 / 512.0)), 8, dtype=np.float32)
    return (np.float32(1.0) - np.exp(lin).astype(np.float32)).astype(np.float64)


ENGINES = ("te", "ve", "se", "sy", "ge")
DMA_ROT = {"XD": ["XD0", "XD1", "XD2"], "OD": ["OD0", "OD1", "OD2"],
           "WD": ["WD0", "WD1"], "CD": ["CD0"]}
SEMKEYS = ("te", "ve", "se", "sy", "ge",
           "XD0", "XD1", "XD2", "OD0", "OD1", "OD2", "WD0", "WD1", "CD0")


class Step:
    __slots__ = ("eng", "fn", "deps", "sig", "dma")

    def __init__(self, eng, fn, deps, sig, dma):
        self.eng, self.fn, self.deps, self.sig, self.dma = eng, fn, deps, sig, dma


class Sched:
    def __init__(self):
        self.steps = []
        self.evs = {}
        self.cnt = {k: 0 for k in SEMKEYS}
        self.last = {e: None for e in ENGINES}
        self.barrier_deps = []
        self.pending_barrier = {e: False for e in ENGINES}
        self.auto = 0
        self.dma_rr = {k: 0 for k in DMA_ROT}
        self.dma_last = {k: None for k in SEMKEYS}

    def _autoname(self):
        self.auto += 1
        return f"_a{self.auto}"

    def add(self, eng, fn, deps=(), sig=None, dma=None):
        assert eng in ENGINES
        deps = [d for d in deps if d is not None]
        if self.pending_barrier[eng]:
            deps = list(self.barrier_deps) + deps
            self.pending_barrier[eng] = False
        if dma is not None:
            stream = dma
            semk = DMA_ROT[stream][self.dma_rr[stream] % len(DMA_ROT[stream])]
            self.dma_rr[stream] += 1
            if self.dma_last[semk] is not None:
                deps = deps + [self.dma_last[semk]]
            self.cnt[semk] += 16
            if sig is None or sig is True:
                sig = self._autoname()
            self.evs[sig] = (semk, self.cnt[semk])
            self.dma_last[semk] = sig
            dma = semk
        elif sig is not None:
            if sig is True:
                sig = self._autoname()
            self.cnt[eng] += 1
            self.evs[sig] = (eng, self.cnt[eng])
        for d in deps:
            assert d in self.evs, f"dep {d} undefined"
        st = Step(eng, fn, deps, sig, dma)
        self.steps.append(st)
        self.last[eng] = st
        return sig

    def barrier(self):
        deps = []
        for e in ENGINES:
            st = self.last[e]
            if st is None:
                continue
            if st.sig is None:
                sig = self._autoname()
                semk = st.dma if st.dma is not None else st.eng
                self.cnt[semk] += 16 if st.dma is not None else 1
                self.evs[sig] = (semk, self.cnt[semk])
                st.sig = sig
            deps.append(st.sig)
        self.barrier_deps = deps
        for e in ENGINES:
            self.pending_barrier[e] = True

    def runner(self, sems):
        import collections
        hwm = {e: collections.defaultdict(int) for e in ENGINES}

        def run(eng_name, eng):
            for st in self.steps:
                if st.eng != eng_name:
                    continue
                for d in st.deps:
                    semk, val = self.evs[d]
                    if hwm[eng_name][semk] >= val:
                        continue
                    eng.wait_ge(sems[semk], val)
                    hwm[eng_name][semk] = val
                inst = st.fn(eng)
                if st.sig is not None:
                    semk, val = self.evs[st.sig]
                    inst.then_inc(sems[semk], 16 if st.dma is not None else 1)
        return run


def build(debug_taps=None):
    debug_taps = debug_taps or {}
    nc = bass.Bass("TRN2", target_bir_lowering=False)
    gam = _gammas()

    x_hbm = nc.dram_tensor("x", [NSEQ, L, D], F32, kind="ExternalInput").ap()
    wt = {}
    for name in ("wq", "wk", "wv", "wo", "fq", "fk", "fv", "fo"):
        wt[name] = nc.dram_tensor(name, [D, D], F32, kind="ExternalInput").ap()
    bp_hbm = nc.dram_tensor("bpack", [128, 49], F32, kind="ExternalInput").ap()
    ob_hbm = nc.dram_tensor("ob_row", [1, D], F32, kind="ExternalInput").ap()
    vb_hbm = nc.dram_tensor("vb_row", [1, D], F32, kind="ExternalInput").ap()
    id_hbm = nc.dram_tensor("ident", [128, 128], BF16, kind="ExternalInput").ap()
    onesb_hbm = nc.dram_tensor("ones_bc", [128, 1], BF16, kind="ExternalInput").ap()
    onesrb_hbm = nc.dram_tensor("ones_rb", [1, 128], BF16, kind="ExternalInput").ap()
    onesf_hbm = nc.dram_tensor("ones_rf", [1, 128], F32, kind="ExternalInput").ap()
    onesfc_hbm = nc.dram_tensor("ones_cf", [128, 1], F32, kind="ExternalInput").ap()
    ghq_hbm = nc.dram_tensor("ghq", [128, 1024], BF16, kind="ExternalInput").ap()
    mask_hbm = nc.dram_tensor("maskc", [128, 1024], BF16, kind="ExternalInput").ap()
    kdec_hbm = nc.dram_tensor("kdec", [128, 1024], BF16, kind="ExternalInput").ap()
    out_hbm = nc.dram_tensor("out", [NSEQ, L, D], F32, kind="ExternalOutput").ap()

    es = contextlib.ExitStack()
    T = lambda *a, **k: es.enter_context(nc.sbuf_tensor(*a, **k))
    SEM = lambda n: es.enter_context(nc.semaphore(n))

    BUF_A = T("buf_a", [128, 16384], BF16)
    BUF_B = T("buf_b", [128, 8192], BF16)
    BUF_D = T("buf_d", [128, 16384], BF16)
    RETK = T("retk", [128, 8192], BF16)
    ATT = RETK
    BUF_E = T("buf_e", [128, 16384], BF16)
    Z_FM = T("z_fm", [128, PACKW], BF16)
    W1 = T("w1", [128, 8192], BF16)
    W2 = T("w2", [128, 8192], BF16)
    XST = [BUF_D[:, 8192 + i * 2048: 8192 + (i + 1) * 2048].bitcast(F32) for i in range(4)]
    XST2 = [BUF_D[:, 0:2048].bitcast(F32), BUF_D[:, 2048:4096].bitcast(F32)]
    TM1D = [BUF_D[:, 4096:5120], BUF_D[:, 5120:6144]]
    TM2D = [BUF_D[:, 6144:7168], BUF_D[:, 7168:8192]]
    OSTG = Z_FM[:, 0:2048].bitcast(F32)
    XSTD = [Z_FM[:, 2048:4096].bitcast(F32), Z_FM[:, 4096:6144].bitcast(F32)]
    SQD = BUF_E[:, 0:1024]
    PBUF2 = [T(f"pbuf2{i}", [128, 1024], BF16) for i in range(2)]
    PBUF = [PBUF2[0][:, 0:512], PBUF2[0][:, 512:1024]]
    KBAR = [PBUF2[1][:, 0:512], PBUF2[1][:, 512:1024]]
    SF32 = T("sf32", [128, 512], F32)
    SQD_GN = SF32[:, :].bitcast(BF16)
    SBF = T("sbf", [128, 512], BF16)
    ZB = T("zb", [128, 512], F32)
    ZROW = T("zrow", [1, 512], F32)
    ZINV = T("zinv", [1, 512], F32)
    ID_SB = T("id_sb", [128, 128], BF16)
    ONESB = T("onesb", [128, 1], BF16)
    ONESRB = T("onesrb", [1, 128], BF16)
    ONESF = T("onesf", [1, 128], F32)
    ONESFC = T("onesfc", [128, 1], F32)
    GHQ = T("ghq_sb", [128, 1024], BF16)
    MASKC = T("mask_sb", [128, 1024], BF16)
    KDEC = T("kdec_sb", [128, 1024], BF16)
    BP = T("bp_sb", [128, 49], F32)
    NBCOL = T("nbcol", [128, 8], BF16)
    OBROW = T("obrow", [1, D], BF16)
    VBROW = T("vbrow", [1, D], F32)
    MU = [T(f"mu{s}", [128, 16], F32) for s in range(NSEQ)]
    RR = [T(f"rr{s}", [128, 16], F32) for s in range(NSEQ)]
    ZS = [T(f"zs{s}", [128, 16], F32) for s in range(NSEQ)]
    NMR = [T(f"nmr{s}", [128, 16], F32) for s in range(NSEQ)]
    NMZ = [T(f"nmz{s}", [128, 16], F32) for s in range(NSEQ)]
    ONEPR = [T(f"onepr{s}", [128, 16], F32) for s in range(NSEQ)]
    TMPA = T("tmpa", [128, 16], F32)
    TMPB = T("tmpb", [128, 16], F32)
    TMPC = T("tmpc", [128, 16], F32)
    YSUM = T("ysum", [128, 16], F32)
    GROW = [T(f"grow{i}", [1, 16], F32) for i in range(6)]
    GSC = T("gsc", [128, 8], F32)
    GBI = T("gbi", [128, 8], F32)
    BQ = T("bq", [128, len(LIVE)], F32)
    BK = T("bk", [128, 32], F32)
    VROWCUR = T("vrowcur", [1, D], BF16)
    GBROW = VROWCUR
    GBIB = T("gbib", [128, 8], BF16)

    PSALL = es.enter_context(nc.psum_tensor("psall", [128, 4096], F32))

    def bank(i, lo=0, w=512):
        return PSALL[:, i * 512 + lo: i * 512 + lo + w]
    PST1 = PSALL[:, 7 * 512: 8 * 512].bitcast(BF16)
    PSB7 = PSALL[:, 7 * 512: 8 * 512]
    PST3 = PSALL[:, 6 * 512: 7 * 512].bitcast(BF16)
    PST2 = PSALL[:, 6 * 512 + 256: 7 * 512].bitcast(BF16)
    PSC = PSALL[:, 6 * 512: 6 * 512 + 256]
    PSZ = PSALL[:, 5 * 512: 6 * 512]

    sems = {k: SEM(k) for k in SEMKEYS}
    S = Sched()

    def vchain(eng, fns, deps=()):
        ev = None
        for i, fn in enumerate(fns):
            ev = S.add(eng, fn, deps=(list(deps) + [ev]) if i == 0 else [ev], sig=True)
        return ev

    taps = {}
    for name, shape in debug_taps.items():
        dt = F32 if (len(shape) > 2 and shape[2] == "f32") else BF16
        taps[name] = nc.dram_tensor("tap_" + name, list(shape[:2]), dt,
                                    kind="ExternalOutput").ap()

    def tap(name, buf_ap):
        if name in taps:
            S.add("sy", lambda e, o=taps[name], i=buf_ap: e.dma_start(out=o, in_=i),
                  dma="OD")

    for dst, src in ((ID_SB, id_hbm), (ONESB, onesb_hbm), (ONESRB, onesrb_hbm),
                     (ONESF, onesf_hbm), (ONESFC, onesfc_hbm),
                     (GHQ, ghq_hbm), (MASKC, mask_hbm), (KDEC, kdec_hbm),
                     (BP, bp_hbm), (VBROW, vb_hbm)):
        S.add("sy", lambda e, d=dst, s=src: e.dma_start(out=d[:], in_=s), dma="CD")
    S.add("ge", lambda e: e.dma_start(out=OBROW[:], in_=ob_hbm), dma="WD")
    S.barrier()
    S.add("ve", lambda e: e.tensor_copy(out=NBCOL[:], in_=BP[:, 16:24]), sig=True)
    S.barrier()

    def load_w(dst_sb, src_name, cols=None, deps=()):
        src_ap = wt[src_name]
        if cols is None:
            src = src_ap.rearrange("(c p) o -> p c o", p=128)
            dst = dst_sb[:, 0:8192].rearrange("p (c o) -> p c o", c=8)
        else:
            lo, wdt = cols
            src = src_ap[:, lo:lo + wdt].rearrange("(c p) o -> p c o", p=128)
            dst = dst_sb[:, 0:8 * wdt].rearrange("p (c o) -> p c o", c=8)
        return S.add("ge", lambda e, d=dst, s=src: e.dma_start(out=d, in_=s),
                     deps=deps, dma="WD")

    for s in range(NSEQ):
        ev_red, ev_sq, ev_chain = {}, {}, {}
        ev_xn_ev, ev_z_ev, ev_xn2, ev_z2, ev_trx, ev_trz2 = {}, {}, {}, {}, {}, {}

        def emit_a1(tc):
            ld = S.add("sy", lambda e, s_=s, t_=tc, b=XST[tc % 4]:
                       e.dma_start(out=b, in_=x_hbm[s_, t_ * 128:(t_ + 1) * 128, :]),
                       deps=[ev_red.get(tc - 4), ev_sq.get(tc - 4),
                             ev_xn2.get(tc - 4), ev_z2.get(tc - 4)], dma="XD")
            ev_red[tc] = S.add("ve", lambda e, b=XST[tc % 4], s_=s, t_=tc:
                               e.tensor_reduce(out=MU[s_][:, t_:t_ + 1], in_=b,
                                               axis=AX.X, op=ALU.add),
                               deps=[ld], sig=True)
            ev_sq[tc] = S.add("se", lambda e, b=XST[tc % 4], s_=s, t_=tc:
                              e.activation(out=SQD, in_=b, func=AF.Square,
                                           accum_out=RR[s_][:, t_:t_ + 1]),
                              deps=[ld, ev_sq.get(tc - 1)], sig=True)

        def emit_chain(grp):
            c0, c1 = grp * 2, grp * 2 + 2
            mu_, sq_ = MU[s][:, c0:c1], RR[s][:, c0:c1]
            ta, tb, tcx = TMPA[:, c0:c1], TMPB[:, c0:c1], TMPC[:, c0:c1]
            r_, zs_ = RR[s][:, c0:c1], ZS[s][:, c0:c1]
            nmr_, nmz_ = NMR[s][:, c0:c1], NMZ[s][:, c0:c1]
            evc = vchain("ve", [
                lambda e, o=mu_: e.tensor_scalar(out=o, in0=o, scalar1=1.0 / 1024.0,
                                                 scalar2=None, op0=ALU.mult),
                lambda e, o=sq_: e.tensor_scalar(out=o, in0=o, scalar1=1.0 / 1024.0,
                                                 scalar2=EPS, op0=ALU.mult, op1=ALU.add),
                lambda e, o=ta, i=mu_: e.tensor_tensor(out=o, in0=i, in1=i, op=ALU.mult),
                lambda e, o=ta, i=sq_: e.tensor_tensor(out=o, in0=i, in1=o, op=ALU.subtract),
            ], deps=[ev_red[c0], ev_red[c0 + 1], ev_sq[c0], ev_sq[c0 + 1]])
            evs = S.add("se", lambda e, i=ta, o=tb:
                        e.activation(out=o, in_=i, func=AF.Sqrt),
                        deps=[evc], sig=True)
            evc2 = vchain("ve", [
                lambda e, o=tcx, i=tb: e.reciprocal(out=o, in_=i),
                lambda e, o=tcx, i=ta: e.scalar_tensor_tensor(out=o, in0=i, scalar=0.5,
                                                              in1=o, op0=ALU.mult, op1=ALU.mult),
                lambda e, o=tb, i=tcx: e.scalar_tensor_tensor(out=o, in0=o, scalar=0.5,
                                                              in1=i, op0=ALU.mult, op1=ALU.add),
                lambda e, o=r_, i=tb: e.reciprocal(out=o, in_=i),
                lambda e, o=tb, i=ta: e.reciprocal(out=o, in_=i),
                lambda e, o=tb: e.tensor_scalar(out=o, in0=o, scalar1=-EPS,
                                                scalar2=1.0 + EPS, op0=ALU.mult, op1=ALU.add),
            ], deps=[evs])
            evs2 = S.add("se", lambda e, i=tb, o=tcx:
                         e.activation(out=o, in_=i, func=AF.Sqrt),
                         deps=[evc2], sig=True)
            ev_chain[grp] = vchain("ve", [
                lambda e, o=ta, i=tcx: e.reciprocal(out=o, in_=i),
                lambda e, o=ta, i=tb: e.scalar_tensor_tensor(out=o, in0=i, scalar=0.5,
                                                             in1=o, op0=ALU.mult, op1=ALU.mult),
                lambda e, o=tcx, i=ta: e.scalar_tensor_tensor(out=o, in0=o, scalar=0.5,
                                                              in1=i, op0=ALU.mult, op1=ALU.add),
                lambda e, o=tcx: e.reciprocal(out=o, in_=o),
                lambda e, o=zs_, a=r_, b=tcx: e.tensor_tensor(out=o, in0=a, in1=b, op=ALU.mult),
                lambda e, o=nmr_, a=mu_, b=r_: e.scalar_tensor_tensor(
                    out=o, in0=a, scalar=-1.0, in1=b, op0=ALU.mult, op1=ALU.mult),
                lambda e, o=nmz_, a=mu_, b=zs_: e.scalar_tensor_tensor(
                    out=o, in0=a, scalar=-1.0, in1=b, op0=ALU.mult, op1=ALU.mult),
            ], deps=[evs2])

        def emit_a2(tc):
            e_seg = tc // 4
            nch = 8 >> e_seg
            pst_x = PST1 if tc % 2 == 0 else PST3
            pst_z = PST3 if tc % 2 == 0 else PST1
            tm1, tm2 = TM1D[tc % 2], TM2D[tc % 2]
            ld = S.add("sy", lambda e, s_=s, t_=tc, b=XST2[tc % 2]:
                       e.dma_start(out=b, in_=x_hbm[s_, t_ * 128:(t_ + 1) * 128, :]),
                       deps=[ev_chain[tc // 2], ev_xn2.get(tc - 2), ev_z2.get(tc - 2)],
                       dma="XD")
            ev_xn2[tc] = S.add("se", lambda e, b=XST[tc % 4], s_=s, t_=tc, o=tm1:
                               e.activation(out=o, in_=b, func=AF.Identity,
                                            scale=RR[s_][:, t_:t_ + 1],
                                            bias=NMR[s_][:, t_:t_ + 1]),
                               deps=[ld, ev_trx.get(tc - 2)], sig=True)
            ev_z2[tc] = S.add("se", lambda e, b=XST[tc % 4], s_=s, t_=tc, o=tm2:
                              e.activation(out=o, in_=b, func=AF.Identity,
                                           scale=ZS[s_][:, t_:t_ + 1],
                                           bias=NMZ[s_][:, t_:t_ + 1]),
                              deps=[ld, ev_trz2.get(tc - 2)], sig=True)
            ev_tr = None
            for ch in range(8):
                ev_tr = S.add("te", lambda e, ch_=ch, pp=pst_x, i=tm1:
                              e.transpose(out=pp[:, ch_ * 128:(ch_ + 1) * 128],
                                          in_=i[:, ch_ * 128:(ch_ + 1) * 128],
                                          identity=ID_SB[:]),
                              deps=[ev_xn2[tc], ev_xn_ev.get(tc - 2),
                                    ev_z_ev.get(tc - 1)] if ch == 0 else [],
                              sig=True if ch == 7 else None)
            ev_trx[tc] = ev_tr
            xn_out = BUF_A[:, :].rearrange("p (c f) -> p c f", c=8)[:, :, tc * 128:(tc + 1) * 128]
            ev_xn_ev[tc] = S.add("ve", lambda e, o=xn_out, pp=pst_x:
                                 e.tensor_copy(out=o,
                                               in_=pp[:, :].rearrange("p (c f) -> p c f", c=8)),
                                 deps=[ev_tr], sig=True)
            ev_tz = None
            for ch in range(nch):
                ev_tz = S.add("te", lambda e, ch_=ch, pp=pst_z, i=tm2:
                              e.transpose(out=pp[:, ch_ * 128:(ch_ + 1) * 128],
                                          in_=i[:, ch_ * 128:(ch_ + 1) * 128],
                                          identity=ID_SB[:]),
                              deps=[ev_z2[tc], ev_z_ev.get(tc - 2),
                                    ev_xn_ev.get(tc - 1)] if ch == 0 else [],
                              sig=True if ch == nch - 1 else None)
            ev_trz2[tc] = ev_tz
            zev = None
            for ch in range(nch):
                off = BLK[(ch, e_seg)] * 512 + (tc % 4) * 128
                zev = S.add("ve", lambda e, ch_=ch, off_=off, pp=pst_z:
                            e.tensor_scalar(out=Z_FM[:, off_:off_ + 128],
                                            in0=pp[:, ch_ * 128:(ch_ + 1) * 128],
                                            scalar1=BP[:, 24 + ch_:25 + ch_], scalar2=None,
                                            op0=ALU.mult),
                            deps=[ev_tz] if ch == 0 else [],
                            sig=True if ch == nch - 1 else None)
            ev_z_ev[tc] = zev
        for u in range(NTC + 2):
            if u < NTC:
                emit_a1(u)
            if u % 2 == 1 and u < NTC:
                emit_chain((u - 1) // 2)
            if u >= 2:
                emit_a2(u - 2)
            for _ in range(20):
                S.add("te", lambda e: e.matmul(out=bank(4, 0, 128), lhsT=ID_SB[:],
                                               rhs=ID_SB[:], start=True, stop=True))


        tap(f"xn_fm_{s}", BUF_A[:, :])
        tap(f"z_fm_{s}", Z_FM[:, :])
        tap(f"zs_{s}", ZS[s][:, :])
        tap(f"rr_{s}", RR[s][:, :])
        tap(f"mu_{s}", MU[s][:, :])

        ev_wq_done = ev_wk_done = ev_wv_done = None
        preload = {}

        def load_qkv(g_):
            olo_ = g_ * 512
            evq_ = load_w(W1, "wq", cols=(olo_, 512), deps=[ev_wq_done, ev_wk_done])
            evk_ = S.add("ge", lambda e, o_=olo_:
                         e.dma_start(out=W1[:, 4096:8192].rearrange("p (c o) -> p c o", c=8),
                                     in_=wt["wk"][:, o_:o_ + 512].rearrange("(c p) o -> p c o", p=128)),
                         deps=[ev_wq_done, ev_wk_done], dma="WD")
            evv_ = load_w(W2, "wv", cols=(olo_, 512), deps=[ev_wv_done])
            return evq_, evk_, evv_

        for g in range(2):
            evq, evk, evv = preload.pop(g) if g in preload else load_qkv(g)
            ev_lastuse = {}
            ev_qev, ev_kev, ev_vev = {}, {}, {}
            for which, w0, ev_w in (("q", 0, evq), ("k", 4096, evk)):
                for hl in range(4):
                    for tpass in range(2):
                        for kc in range(8):
                            for tb in (tpass * 2, tpass * 2 + 1):
                                pb = bank(2 + tb % 2)
                                dep = []
                                if kc == 0 and tb % 2 == 0:
                                    dep = [ev_w]
                                if kc == 0:
                                    dep.append(ev_lastuse.get(tb % 2))
                                    dep.append(ev_xn_ev.get(tb * 4 + 3))
                                ev_mm = S.add(
                                    "te", lambda e, w0_=w0, hl_=hl, kc_=kc, tb_=tb, pb=pb:
                                    e.matmul(out=pb,
                                             lhsT=W1[:, w0_ + kc_ * 512 + hl_ * 128:
                                                     w0_ + kc_ * 512 + (hl_ + 1) * 128],
                                             rhs=BUF_A[:, kc_ * 2048 + tb_ * 512:
                                                       kc_ * 2048 + (tb_ + 1) * 512],
                                             start=(kc_ == 0), stop=(kc_ == 7)),
                                    deps=dep, sig=True if kc == 7 else None)
                                if kc == 7:
                                    if which == "q":
                                        h = g * 4 + hl
                                        ghq_rep = GHQ[:, h * 128:(h + 1) * 128] \
                                            .unsqueeze(1).broadcast_to([128, 4, 128])
                                        ev = S.add(
                                            "ve", lambda e, hl_=hl, tb_=tb, gr=ghq_rep, pb=pb:
                                            e.tensor_tensor(
                                                out=BUF_B[:, hl_ * 2048 + tb_ * 512:
                                                          hl_ * 2048 + (tb_ + 1) * 512]
                                                .rearrange("p (c f) -> p c f", c=4),
                                                in0=pb.rearrange("p (c f) -> p c f", c=4),
                                                in1=gr, op=ALU.mult),
                                            deps=[ev_mm], sig=True)
                                        ev_qev[(hl, tb)] = ev
                                    else:
                                        ev = S.add(
                                            "se", lambda e, hl_=hl, tb_=tb, pb=pb:
                                            e.activation(out=RETK[:, hl_ * 2048 + tb_ * 512:
                                                                 hl_ * 2048 + (tb_ + 1) * 512],
                                                         in_=pb, func=AF.Identity),
                                            deps=[ev_mm], sig=True)
                                        ev_kev[(hl, tb)] = ev
                                    ev_lastuse[tb % 2] = ev
            if g == 0:
                ev_wq_done = ev_kev[(3, 3)]
                ev_wk_done = ev_wq_done
            for tc in range(NTC):
                pb = bank(2 + tc % 2)
                for kc in range(8):
                    dep = []
                    if kc == 0:
                        dep = [evv, ev_lastuse.get(tc % 2), ev_xn_ev.get(tc)]
                    ev_mm = S.add("te", lambda e, kc_=kc, tc_=tc, pb=pb:
                                  e.matmul(out=pb,
                                           lhsT=BUF_A[:, kc_ * 2048 + tc_ * 128:
                                                     kc_ * 2048 + (tc_ + 1) * 128],
                                           rhs=W2[:, kc_ * 512:(kc_ + 1) * 512],
                                           start=(kc_ == 0), stop=(kc_ == 7)),
                                  deps=dep, sig=True if kc == 7 else None)
                ev = S.add("se", lambda e, tc_=tc, pb=pb:
                           e.activation(out=BUF_D[:, tc_ * 512:(tc_ + 1) * 512],
                                        in_=pb, func=AF.Identity),
                           deps=[ev_mm], sig=True)
                ev_vev[tc] = ev
                ev_lastuse[tc % 2] = ev
            if g == 0:
                ev_wv_done = ev_vev[15]
            ev_bprojlast = ev_mm
            if g == 0:
                preload[1] = load_qkv(1)

            ev_sinit = S.add("ve", lambda e: e.memset(SF32[:], 0.0), sig=True)
            ev_sup = {-1: S.add("se", lambda e: e.copy(out=SBF[:], in_=SF32[:]),
                                deps=[ev_sinit], sig=True)}
            ev_mask_prev = None
            ev_kbar, ev_ds, ev_ybd = {}, {}, {}
            ev_y_ev = None
            for c in range(NTC):
                ev_sc = None
                prj_deps = [ev_qev.get((hl2, c // 4)) for hl2 in range(4)] + \
                           [ev_kev.get((hl2, c // 4)) for hl2 in range(4)]
                for hl in range(4):
                    ev_sc = S.add("te", lambda e, hl_=hl, c_=c:
                                  e.matmul(out=bank(1, hl_ * 128, 128),
                                           lhsT=RETK[:, hl_ * 2048 + c_ * 128:
                                                    hl_ * 2048 + (c_ + 1) * 128],
                                           rhs=BUF_B[:, hl_ * 2048 + c_ * 128:
                                                     hl_ * 2048 + (c_ + 1) * 128],
                                           start=(hl_ == 0), stop=(hl_ == 3)),
                                  deps=([ev_mask_prev] + prj_deps) if hl == 0 else [],
                                  sig=True if hl == 3 else None)
                mrep = MASKC[:, g * 512:(g + 1) * 512]
                ev_mask = S.add("ve", lambda e, c_=c, mr=mrep:
                                e.tensor_tensor(out=PBUF[c_ % 2], in0=bank(1),
                                                in1=mr, op=ALU.mult),
                                deps=[ev_sc, ev_ybd.get(c - 2)], sig=True)
                ev_mask_prev = ev_mask
                pst = PST1[:, 0:512] if c % 2 == 0 else PST2
                ev_ktr = None
                for hl in range(4):
                    tr_dep = [ev_kbar.get(c - 2)]
                    if c < 2 and g == 0:
                        tr_dep += [ev_z_ev.get(NTC - 1), ev_xn_ev.get(NTC - 1)]
                    ev_ktr = S.add("te", lambda e, hl_=hl, c_=c, p=pst:
                                   e.transpose(out=p[:, hl_ * 128:(hl_ + 1) * 128],
                                               in_=RETK[:, hl_ * 2048 + c_ * 128:
                                                       hl_ * 2048 + (c_ + 1) * 128],
                                               identity=ID_SB[:]),
                                   deps=tr_dep if hl == 0 else [],
                                   sig=True if hl == 3 else None)
                kd = KDEC[:, g * 512:(g + 1) * 512]
                ev_kbar[c] = S.add("ve", lambda e, c_=c, p=pst, kd_=kd:
                                   e.tensor_tensor(out=KBAR[c_ % 2], in0=p,
                                                   in1=kd_, op=ALU.mult),
                                   deps=[ev_ktr, ev_ds.get(c - 2)], sig=True)
                if c > 0:
                    for _ in range(6):
                        S.add("te", lambda e: e.matmul(out=bank(4, 0, 128), lhsT=ID_SB[:],
                                                       rhs=ID_SB[:], start=True, stop=True))
                    for hl in range(4):
                        S.add("te", lambda e, hl_=hl, c_=c:
                              e.matmul(out=bank(0, hl_ * 128, 128),
                                       lhsT=SBF[:, hl_ * 128:(hl_ + 1) * 128],
                                       rhs=BUF_B[:, hl_ * 2048 + c_ * 128:
                                                 hl_ * 2048 + (c_ + 1) * 128],
                                       start=(hl_ == 0), stop=False),
                              deps=[ev_sup[c - 1], ev_y_ev] if hl == 0 else [])
                ev_yb = None
                for hl in range(4):
                    st = (c == 0 and hl == 0)
                    ev_yb = S.add("te", lambda e, hl_=hl, c_=c, st_=st:
                                  e.matmul(out=bank(0, hl_ * 128, 128),
                                           lhsT=BUF_D[:, c_ * 512 + hl_ * 128:
                                                     c_ * 512 + (hl_ + 1) * 128],
                                           rhs=PBUF[c_ % 2][:, hl_ * 128:(hl_ + 1) * 128],
                                           start=st_, stop=(hl_ == 3)),
                                  deps=[ev_mask, ev_y_ev, ev_vev.get(c)] if hl == 0 else [],
                                  sig=True if hl == 3 else None)
                ev_ybd[c] = ev_yb
                ev_d = None
                for hl in range(4):
                    ev_d = S.add("te", lambda e, hl_=hl, c_=c:
                                 e.matmul(out=PSZ[:, hl_ * 128:(hl_ + 1) * 128],
                                          lhsT=KBAR[c_ % 2][:, hl_ * 128:(hl_ + 1) * 128],
                                          rhs=BUF_D[:, c_ * 512 + hl_ * 128:
                                                    c_ * 512 + (hl_ + 1) * 128],
                                          start=(hl_ == 0), stop=(hl_ == 3)),
                                 deps=[ev_kbar[c], ev_ds.get(c - 1)] if hl == 0 else [],
                                 sig=True if hl == 3 else None)

                def supd(e, g_=g):
                    r = None
                    for hl in range(4):
                        gc = float(gam[g_ * 4 + hl] ** 128)
                        r = e.scalar_tensor_tensor(out=SF32[:, hl * 128:(hl + 1) * 128],
                                                   in0=SF32[:, hl * 128:(hl + 1) * 128],
                                                   scalar=gc,
                                                   in1=PSZ[:, hl * 128:(hl + 1) * 128],
                                                   op0=ALU.mult, op1=ALU.add)
                    return r
                ev_sf = S.add("ve", supd, deps=[ev_d, ev_sup[c - 1]], sig=True)
                ev_ds[c] = ev_sf
                ev_sup[c] = S.add("se", lambda e: e.copy(out=SBF[:], in_=SF32[:]),
                                  deps=[ev_sf], sig=True)
                y_out = BUF_E[:, :].rearrange("p (c f) -> p c f", c=8)[
                    :, g * 4:g * 4 + 4, c * 128:(c + 1) * 128]
                ev_y_ev = S.add("se", lambda e, yo=y_out:
                                e.activation(out=yo,
                                             in_=bank(0).rearrange("p (c f) -> p c f", c=4),
                                             func=AF.Identity),
                                deps=[ev_yb], sig=True)
            if g == 1:
                ev_rec_end = S.last["te"].sig
                assert ev_rec_end is not None
                ev_yev_last = ev_y_ev
                ev_mask_last = ev_mask_prev
                ev_supd_last = ev_ds[NTC - 1]
            else:
                S.barrier()

        tap(f"y_fm_{s}", BUF_E[:, :])

        ev_kw = load_w(W2, "fk", deps=[ev_bprojlast])
        ev_qw = load_w(W1, "fq", deps=[ev_bprojlast])
        PSCB = PSALL[:, 4 * 512: 4 * 512 + 256]
        ev_bfree = {}
        ev_beff_prev = None
        ev_bq = {}
        ev_qfe, ev_kevF = {}, {}
        for h in range(8):
            bcnt = 0
            nbeff = 15
            for e_ in range(4):
                nch = 8 >> e_
                pb = bank(2 + e_ % 2)
                ev_mm = None
                for kc in range(nch):
                    dep = []
                    if kc == 0:
                        dep = [ev_kw, ev_bfree.get(2 + e_ % 2), ev_z_ev.get(NTC - 1)]
                    ev_mm = S.add(
                        "te", lambda e, h_=h, kc_=kc, e2=e_, nch_=nch, pb=pb:
                        e.matmul(out=pb,
                                 lhsT=W2[:, kc_ * 1024 + h_ * 128:kc_ * 1024 + (h_ + 1) * 128],
                                 rhs=Z_FM[:, BLK[(kc_, e2)] * 512:(BLK[(kc_, e2)] + 1) * 512],
                                 start=(kc_ == 0), stop=(kc_ == nch_ - 1)),
                        deps=dep, sig=True if kc == nch - 1 else None)
                    if s == 0:
                        bcnt += 1
                        S.add("te", lambda e, h_=h, kc_=kc, e2=e_, st_=(bcnt == 1),
                              sp_=(bcnt == nbeff):
                              e.matmul(out=PSCB[:, e2:e2 + 1],
                                       lhsT=W2[:, kc_ * 1024 + h_ * 128:
                                               kc_ * 1024 + (h_ + 1) * 128],
                                       rhs=NBCOL[:, kc_:kc_ + 1],
                                       start=st_, stop=sp_),
                              deps=[ev_beff_prev] if bcnt == 1 else [],
                              sig=True if bcnt == nbeff else None)
                        if bcnt == nbeff:
                            ev_blast = S.steps[-1].sig
                ev_bfree[("kmm", h, e_)] = ev_mm
                if s != 0:
                    ev = S.add("se", lambda e, h_=h, e2=e_, pb=pb:
                               e.activation(out=BUF_A[:, h_ * 2048 + e2 * 512:
                                                    h_ * 2048 + (e2 + 1) * 512],
                                            in_=pb, func=AF.Identity,
                                            bias=BK[:, h_ * 4 + e2:h_ * 4 + e2 + 1]),
                               deps=[ev_mm, ev_bprojlast], sig=True)
                    ev_bfree[2 + e_ % 2] = ev
                    ev_kevF[h] = ev
            if s == 0:
                evl = None
                for e_ in range(4):
                    evl = S.add("ve", lambda e, h_=h, e2=e_:
                                e.tensor_scalar(out=BK[:, h_ * 4 + e2:h_ * 4 + e2 + 1],
                                                in0=PSCB[:, e2:e2 + 1],
                                                scalar1=BP[:, 8 + h_:9 + h_], scalar2=None,
                                                op0=ALU.add),
                                deps=[ev_blast], sig=True)
                    ev_bq[("k", h, e_)] = evl
                ev_beff_prev = evl
                for e_ in range(4):
                    ev = S.add("se", lambda e, h_=h, e2=e_, pb=bank(2 + e_ % 2):
                               e.activation(out=BUF_A[:, h_ * 2048 + e2 * 512:
                                                    h_ * 2048 + (e2 + 1) * 512],
                                            in_=pb, func=AF.Identity,
                                            bias=BK[:, h_ * 4 + e2:h_ * 4 + e2 + 1]),
                               deps=[ev_bfree[("kmm", h, e_)], ev_bq[("k", h, e_)],
                                     ev_bprojlast], sig=True)
                    ev_bfree[2 + e_ % 2] = ev
                    ev_kevF[h] = ev
        ev_kdone = S.last["te"].sig
        assert ev_kdone is not None
        ev_vw0 = S.add("ge", lambda e:
                       e.dma_start(out=RETK[:, 0:4096].rearrange("p (c o) -> p c o", c=4),
                                   in_=wt["fv"][0:512, :].rearrange("(c p) o -> p c o", p=128)),
                       deps=[ev_rec_end], dma="WD")
        ev_vw = S.add("ge", lambda e:
                      e.dma_start(out=RETK[:, 4096:8192].rearrange("p (c o) -> p c o", c=4),
                                  in_=wt["fv"][512:1024, :].rearrange("(c p) o -> p c o", p=128)),
                      deps=[ev_rec_end], dma="WD")
        ev_ow = load_w(W2, "fo", deps=[ev_kdone])
        ev_beff_done = ev_beff_prev if s == 0 else None
        for h in range(8):
            lives = [e_ for e_ in range(4) if h < (8 >> e_)]
            maxk = 8 >> min(lives) if lives else 0
            nbeff = sum(1 for e_ in lives for kc in range(8 >> e_))
            bcnt = 0
            for kc in range(maxk):
                for e_ in lives:
                    nch = 8 >> e_
                    if kc >= nch:
                        continue
                    dep = []
                    if kc == 0:
                        dep = [ev_qw, ev_bfree.get(e_)]
                        if h == 0 and e_ < 2:
                            dep += [ev_yev_last, ev_mask_last]
                    ev_mm = S.add(
                        "te", lambda e, h_=h, kc_=kc, e2=e_, nch_=nch:
                        e.matmul(out=bank(e2),
                                 lhsT=W1[:, kc_ * 1024 + h_ * 128:kc_ * 1024 + (h_ + 1) * 128],
                                 rhs=Z_FM[:, BLK[(kc_, e2)] * 512:(BLK[(kc_, e2)] + 1) * 512],
                                 start=(kc_ == 0), stop=(kc_ == nch_ - 1)),
                        deps=dep, sig=True if kc == nch - 1 else None)
                    if s == 0:
                        bcnt += 1
                        S.add("te", lambda e, h_=h, kc_=kc, e2=e_, st_=(bcnt == 1),
                              sp_=(bcnt == nbeff):
                              e.matmul(out=PSCB[:, e2:e2 + 1],
                                       lhsT=W1[:, kc_ * 1024 + h_ * 128:
                                               kc_ * 1024 + (h_ + 1) * 128],
                                       rhs=NBCOL[:, kc_:kc_ + 1],
                                       start=st_, stop=sp_),
                              deps=[ev_beff_prev] if bcnt == 1 else [],
                              sig=True if bcnt == nbeff else None)
                        if bcnt == nbeff:
                            ev_blast = S.steps[-1].sig
                    if kc == nch - 1 and s != 0:
                        ev = S.add("se", lambda e, h_=h, e2=e_:
                                   e.activation(out=BUF_B[:, BLK[(h_, e2)] * 512:
                                                        (BLK[(h_, e2)] + 1) * 512],
                                                in_=bank(e2), func=AF.Identity,
                                                bias=BQ[:, BLK[(h_, e2)]:BLK[(h_, e2)] + 1]),
                                   deps=[ev_mm], sig=True)
                        ev_bfree[e_] = ev
                        ev_qfe[(h, e_)] = ev
                    elif kc == nch - 1:
                        ev_bfree[(h, e_, "mm")] = ev_mm
            if s == 0:
                evl = None
                for e_ in lives:
                    ev_bq[(h, e_)] = S.add(
                        "ve", lambda e, h_=h, e2=e_:
                        e.tensor_scalar(out=BQ[:, BLK[(h_, e2)]:BLK[(h_, e2)] + 1],
                                        in0=PSCB[:, e2:e2 + 1],
                                        scalar1=BP[:, h_:h_ + 1], scalar2=None,
                                        op0=ALU.add),
                        deps=[ev_blast], sig=True)
                    evl = ev_bq[(h, e_)]
                ev_beff_prev = evl
                for e_ in lives:
                    ev = S.add("se", lambda e, h_=h, e2=e_:
                               e.activation(out=BUF_B[:, BLK[(h_, e2)] * 512:
                                                    (BLK[(h_, e2)] + 1) * 512],
                                            in_=bank(e2), func=AF.Identity,
                                            bias=BQ[:, BLK[(h_, e2)]:BLK[(h_, e2)] + 1]),
                               deps=[ev_bfree[(h, e_, "mm")], ev_bq[(h, e_)]], sig=True)
                    ev_bfree[e_] = ev
                    ev_qfe[(h, e_)] = ev
        ev_qdone = S.last["te"].sig
        assert ev_qdone is not None
        ev_wo = load_w(W1, "wo", deps=[ev_qdone])
        ev_r = None
        ev_q = None
        for ch in range(8):
            ev_r = S.add("ve", lambda e, ch_=ch:
                         e.tensor_reduce(out=YSUM[:, ch_:ch_ + 1],
                                         in_=BUF_E[:, ch_ * 2048:(ch_ + 1) * 2048],
                                         axis=AX.X, op=ALU.add),
                         deps=[ev_yev_last] if ch == 0 else [], sig=True)
            for half in range(2):
                dst = TMPA if half == 0 else TMPB
                ev_q = S.add("se", lambda e, ch_=ch, h_=half, d=dst:
                             e.activation(out=SQD_GN,
                                          in_=BUF_E[:, ch_ * 2048 + h_ * 1024:
                                                   ch_ * 2048 + (h_ + 1) * 1024],
                                          func=AF.Square, accum_out=d[:, ch_:ch_ + 1]),
                             deps=[ev_q], sig=True)
        ev_sq2 = S.add("ve", lambda e:
                       e.tensor_tensor(out=YSUM[:, 8:16], in0=TMPA[:, 0:8],
                                       in1=TMPB[:, 0:8], op=ALU.add),
                       deps=[ev_q, ev_r], sig=True)
        n_ = 1.0 / (2048.0 * 128.0)
        r0, r1, r2_, r3, r4, r5 = (GROW[i][0:1, :] for i in range(6))
        gnev = {}

        def emit_gn_a():
            ev_gn0 = S.add("te", lambda e:
                           e.matmul(out=PSC[0:1, 0:16], lhsT=ONESFC[:], rhs=YSUM[:],
                                    start=True, stop=True),
                           deps=[ev_sq2], sig=True)
            ev_gn1 = vchain("ve", [
                lambda e: e.tensor_scalar(out=r0, in0=PSC[0:1, 0:16], scalar1=n_,
                                          scalar2=None, op0=ALU.mult),
                lambda e: e.tensor_tensor(out=r1[:, 0:8], in0=r0[0:1, 0:8],
                                          in1=r0[0:1, 0:8], op=ALU.mult),
                lambda e: e.tensor_scalar(out=r2_[:, 0:8], in0=r0[0:1, 8:16],
                                          scalar1=EPS, scalar2=None, op0=ALU.add),
                lambda e: e.tensor_tensor(out=r1[:, 0:8], in0=r2_[:, 0:8],
                                          in1=r1[:, 0:8], op=ALU.subtract),
            ], deps=[ev_gn0])
            ev_gn2 = S.add("se", lambda e:
                           e.activation(out=GROW[2][0:1, 0:8], in_=GROW[1][0:1, 0:8],
                                        func=AF.Sqrt),
                           deps=[ev_gn1], sig=True)
            gnev["gn3"] = vchain("ve", [
                lambda e: e.reciprocal(out=r3[:, 0:8], in_=r2_[:, 0:8]),
                lambda e: e.scalar_tensor_tensor(out=r3[:, 0:8], in0=r1[:, 0:8], scalar=0.5,
                                                 in1=r3[:, 0:8], op0=ALU.mult, op1=ALU.mult),
                lambda e: e.scalar_tensor_tensor(out=r2_[:, 0:8], in0=r2_[:, 0:8], scalar=0.5,
                                                 in1=r3[:, 0:8], op0=ALU.mult, op1=ALU.add),
                lambda e: e.reciprocal(out=r4[:, 0:8], in_=r2_[:, 0:8]),
                lambda e: e.scalar_tensor_tensor(out=r5[:, 0:8], in0=r0[0:1, 0:8], scalar=-1.0,
                                                 in1=r4[:, 0:8], op0=ALU.mult, op1=ALU.mult),
            ], deps=[ev_gn2])

        def emit_gn_b():
            ev_gn4 = S.add("te", lambda e:
                           e.matmul(out=PSC[:, 32:40], lhsT=ONESF[:], rhs=GROW[4][0:1, 0:8],
                                    start=True, stop=True),
                           deps=[gnev["gn3"]], sig=True)
            ev_gn5 = S.add("te", lambda e:
                           e.matmul(out=PSC[:, 40:48], lhsT=ONESF[:], rhs=GROW[5][0:1, 0:8],
                                    start=True, stop=True),
                           sig=True)
            gnev["cols"] = vchain("ve", [
                lambda e: e.tensor_tensor(out=GSC[:, 0:8], in0=PSC[:, 32:40],
                                          in1=BP[:, 32:40], op=ALU.mult),
                lambda e: e.tensor_tensor(out=GBI[:, 0:8], in0=PSC[:, 40:48],
                                          in1=BP[:, 32:40], op=ALU.mult),
                lambda e: e.tensor_tensor(out=GBI[:, 0:8], in0=GBI[:, 0:8],
                                          in1=BP[:, 40:48], op=ALU.add),
            ], deps=[ev_gn4, ev_gn5])
        ev_vfree = {}
        ev_vrow = None
        for tc in range(NTC):
            if tc == 5:
                emit_gn_a()
            if tc == 7:
                emit_gn_b()
            e_seg = tc // 4
            nch = 8 >> e_seg
            pk = (tc % 2) * 2
            if tc % 4 == 0:
                for half in range(2):
                    ev_mm = None
                    for kc in range(nch):
                        ev_mm = S.add(
                            "te", lambda e, kc_=kc, h_=half, nch_=nch:
                            e.matmul(out=PSZ[0:1, 0:512],
                                     lhsT=NBCOL[:, kc_:kc_ + 1],
                                     rhs=RETK[:, kc_ * 1024 + h_ * 512:
                                              kc_ * 1024 + (h_ + 1) * 512],
                                     start=(kc_ == 0), stop=(kc_ == nch_ - 1)),
                            deps=[ev_vw0, ev_vw, ev_vrow, ev_supd_last] if kc == 0 else [],
                            sig=True if kc == nch - 1 else None)
                    ev_vrow = S.add("ve", lambda e, h_=half:
                                    e.tensor_tensor(out=VROWCUR[0:1, h_ * 512:(h_ + 1) * 512],
                                                    in0=PSZ[0:1, 0:512],
                                                    in1=VBROW[0:1, h_ * 512:(h_ + 1) * 512],
                                                    op=ALU.add),
                                    deps=[ev_mm], sig=True)
            for kc in range(nch):
                for half in range(2):
                    S.add("te", lambda e, kc_=kc, tc_=tc, h_=half, pk_=pk, e2=e_seg:
                          e.matmul(out=bank(pk_ + h_),
                                   lhsT=Z_FM[:, BLK[(kc_, e2)] * 512 + (tc_ % 4) * 128:
                                             BLK[(kc_, e2)] * 512 + (tc_ % 4 + 1) * 128],
                                   rhs=RETK[:, kc_ * 1024 + h_ * 512:kc_ * 1024 + (h_ + 1) * 512],
                                   start=(kc_ == 0), stop=False),
                          deps=[ev_vw0, ev_vw, ev_vfree.get(tc - 2),
                                ev_bfree.get((tc % 2) * 2), ev_bfree.get((tc % 2) * 2 + 1),
                                ev_rec_end if tc < 2 else None]
                          if (kc == 0 and half == 0) else [])
            ev_mm = None
            for half in range(2):
                ev_mm = S.add("te", lambda e, tc_=tc, h_=half, pk_=pk:
                              e.matmul(out=bank(pk_ + h_),
                                       lhsT=ONESRB[:],
                                       rhs=VROWCUR[0:1, h_ * 512:(h_ + 1) * 512],
                                       start=False, stop=True),
                              deps=[ev_vrow] if half == 0 else [],
                              sig=True if half == 1 else None)
            ev_vfree[tc] = S.add("ve", lambda e, tc_=tc, pk_=pk:
                                 e.tensor_copy(out=BUF_D[:, tc_ * 1024:(tc_ + 1) * 1024],
                                               in_=bank(pk_, 0, 1024)),
                                 deps=[ev_mm], sig=True)
        tap(f"qf_{s}", BUF_B[:, 0:PACKW])
        tap(f"kf_{s}", BUF_A[:, :])
        tap(f"vt_{s}", BUF_D[:, :])

        ev_dprep = S.add("ve", lambda e, s_=s:
                         e.tensor_scalar(out=ONEPR[s_][:], in0=RR[s_][:], scalar1=1.0,
                                         scalar2=None, op0=ALU.add), sig=True)
        ev_gbib = S.add("ve", lambda e: e.tensor_copy(out=GBIB[:], in_=GBI[:]),
                        deps=[gnev["cols"]], sig=True)
        ev_grow = None
        for half in range(2):
            ev_mm = None
            for kc in range(8):
                ev_mm = S.add("te", lambda e, kc_=kc, h_=half:
                              e.matmul(out=PSB7[0:1, :],
                                       lhsT=GBIB[:, kc_:kc_ + 1],
                                       rhs=W1[:, kc_ * 1024 + h_ * 512:
                                               kc_ * 1024 + (h_ + 1) * 512],
                                       start=(kc_ == 0), stop=(kc_ == 7)),
                              deps=[ev_wo, ev_gbib, ev_grow,
                                    ev_vfree.get(NTC - 1)] if kc == 0 else [],
                              sig=True if kc == 7 else None)
            ev_grow = S.add("ve", lambda e, h_=half:
                            e.tensor_copy(out=GBROW[0:1, h_ * 512:(h_ + 1) * 512],
                                          in_=PSB7[0:1, 0:512]),
                            deps=[ev_mm], sig=True)
        ev_w1s = None
        for ch in range(8):
            ev_w1s = S.add("ve", lambda e, ch_=ch:
                           e.tensor_scalar(out=W1[:, ch_ * 1024:(ch_ + 1) * 1024],
                                           in0=W1[:, ch_ * 1024:(ch_ + 1) * 1024],
                                           scalar1=GSC[:, ch_:ch_ + 1], scalar2=None,
                                           op0=ALU.mult),
                           deps=[ev_grow, gnev["cols"]] if ch == 0 else [],
                           sig=True if ch == 7 else None)

        ev_exp, ev_z, ev_fold, ev_att, ev_zbe, ev_zbmm = {}, {}, {}, {}, {}, {}
        ev_zmm_d = {}
        ev_out_tt, ev_store, ev_t0_d = {}, {}, {}
        XDEP2 = {0: 6, 1: 7}
        blk_i = 0

        def emit_dtile(tc, ev_att_qb):
            e_seg = tc // 4
            me = D >> e_seg
            nch = 8 >> e_seg
            nho = (me + 511) // 512
            ld = S.add("sy", lambda e, s_=s, t_=tc, b=XSTD[tc % 2]:
                       e.dma_start(out=b, in_=x_hbm[s_, t_ * 128:(t_ + 1) * 128, :]),
                       deps=[ev_t0_d.get(tc - 2),
                             ev_vfree[NTC - 1] if tc < 2 else None], dma="XD")
            for kc in range(8):
                for half in range(2):
                    dep = []
                    if kc == 0 and half == 0:
                        dep = [ev_w1s, ev_out_tt.get(tc - 1), ev_beff_done]
                    S.add("te", lambda e, kc_=kc, h_=half, tc_=tc:
                          e.matmul(out=bank(5 + h_),
                                   lhsT=BUF_E[:, kc_ * 2048 + tc_ * 128:
                                             kc_ * 2048 + (tc_ + 1) * 128],
                                   rhs=W1[:, kc_ * 1024 + h_ * 512:kc_ * 1024 + (h_ + 1) * 512],
                                   start=(kc_ == 0), stop=False),
                          deps=dep)
            for kc in range(nch):
                for half in range(nho):
                    wdt = min(512, me - half * 512)
                    S.add("te", lambda e, kc_=kc, h_=half, tc_=tc, e2=e_seg, w_=wdt:
                          e.matmul(out=bank(5 + h_, 0, w_),
                                   lhsT=ATT[:, BLK[(kc_, e2)] * 512 + (tc_ % 4) * 128:
                                            BLK[(kc_, e2)] * 512 + (tc_ % 4 + 1) * 128],
                                   rhs=W2[:, kc_ * 1024 + h_ * 512:kc_ * 1024 + h_ * 512 + w_],
                                   start=False, stop=False),
                          deps=[ev_ow, ev_att_qb] if (kc == 0 and half == 0) else [])
            for half in range(2):
                S.add("te", lambda e, h_=half, sp_=(half == 1 and me <= 512):
                      e.matmul(out=bank(5 + h_), lhsT=ONESRB[:],
                               rhs=GBROW[0:1, h_ * 512:(h_ + 1) * 512],
                               start=False, stop=sp_), deps=[])
            ev_last = None
            for half in range(nho):
                wdt = min(512, me - half * 512)
                ev_last = S.add("te", lambda e, h_=half, w_=wdt:
                                e.matmul(out=bank(5 + h_, 0, w_), lhsT=ONESRB[:],
                                         rhs=OBROW[0:1, h_ * 512:h_ * 512 + w_],
                                         start=False, stop=(h_ == 0 or me > 512)),
                                sig=True)
            ev_t0 = S.add("se", lambda e, s_=s, t_=tc, b=XSTD[tc % 2]:
                          e.activation(out=OSTG, in_=b, func=AF.Identity,
                                       scale=ONEPR[s_][:, t_:t_ + 1],
                                       bias=NMR[s_][:, t_:t_ + 1]),
                          deps=[ld, ev_dprep, ev_store.get(tc - 1)], sig=True)
            ev_t0_d[tc] = ev_t0
            ev_out_tt[tc] = S.add("ve", lambda e:
                                  e.tensor_tensor(out=OSTG, in0=bank(5, 0, 1024),
                                                  in1=OSTG, op=ALU.add),
                                  deps=[ev_last, ev_t0], sig=True)
            ev_store[tc] = S.add("sy", lambda e, s_=s, t_=tc:
                                 e.dma_start(out=out_hbm[s_, t_ * 128:(t_ + 1) * 128, :],
                                             in_=OSTG),
                                 deps=[ev_out_tt[tc]], dma="OD")

        ev_att_done = {}
        d_next = 0
        for qb in range(4):
            e_seg = qb
            for h in range(8 >> e_seg):
                qoff = BLK[(h, e_seg)] * 512
                for p in range(9):
                    if p < 8:
                        sb = (p % 2) * 2
                        dep_sc = [ev_exp.get((blk_i, p - 2))]
                        if p < 2:
                            dep_sc += [ev_exp.get((blk_i - 1, XDEP2[p])),
                                       ev_kevF.get(h), ev_qfe.get((h, e_seg))]
                            if blk_i == 0:
                                dep_sc.append(ev_vfree[NTC - 1])
                        ev_sc = None
                        for half in range(2):
                            c = 2 * p + half
                            ev_sc = S.add(
                                "te", lambda e, h_=h, c_=c, q0=qoff, b_=sb + half:
                                e.matmul(out=bank(b_),
                                         lhsT=BUF_A[:, h_ * 2048 + c_ * 128:
                                                  h_ * 2048 + (c_ + 1) * 128],
                                         rhs=BUF_B[:, q0:q0 + 512],
                                         start=True, stop=True),
                                deps=dep_sc if half == 0 else [],
                                sig=True if half == 1 else None)
                        ev_exp[(blk_i, p)] = S.add(
                            "se", lambda e, p_=p, b_=sb:
                            e.activation(out=PBUF2[p_ % 2][:],
                                         in_=PSALL[:, b_ * 512:b_ * 512 + 1024],
                                         func=AF.Exp, scale=SM_SCALE, bias=BP[:, 48:49]),
                            deps=[ev_sc, ev_z.get((blk_i, p - 2)),
                                  ev_fold.get((blk_i, p - 2)),
                                  ev_z.get((blk_i - 1, p + 6)),
                                  ev_fold.get((blk_i - 1, p + 6))], sig=True)
                    if p >= 1:
                        pp = p - 1
                        dep_pv = [ev_exp[(blk_i, pp)], ev_vfree.get(2 * pp + 1)]
                        if pp == 0:
                            dep_pv += [ev_att.get(blk_i - 1)]
                        for half in range(2):
                            c = 2 * pp + half
                            S.add("te", lambda e, h_=h, c_=c, p_=pp, hf=half:
                                  e.matmul(out=bank(4),
                                           lhsT=BUF_D[:, c_ * 1024 + h_ * 128:
                                                     c_ * 1024 + (h_ + 1) * 128],
                                           rhs=PBUF2[p_ % 2][:, hf * 512:(hf + 1) * 512],
                                           start=(c_ == 0), stop=(c_ == 15)),
                                  deps=dep_pv if half == 0 else [])
                        dep_z = []
                        if pp == 0:
                            dep_z = [ev_zbe.get(blk_i - 1)]
                            if blk_i == 0:
                                dep_z.append(ev_grow)
                        ev_z[(blk_i, pp)] = S.add(
                            "te", lambda e, p_=pp:
                            e.matmul(out=PSB7[0:1, :], lhsT=ONESB[:],
                                     rhs=PBUF2[p_ % 2][:, 0:512],
                                     start=(p_ == 0), stop=False),
                            deps=dep_z, sig=True)
                        if pp == 0:
                            ev_fold[(blk_i, pp)] = S.add(
                                "ve", lambda e, p_=pp:
                                e.tensor_scalar(out=SF32[:],
                                                in0=PBUF2[p_ % 2][:, 512:1024],
                                                scalar1=1.0, scalar2=None, op0=ALU.mult),
                                deps=[ev_exp[(blk_i, pp)], ev_zmm_d.get(blk_i - 1),
                                      ev_sq2 if blk_i == 0 else None],
                                sig=True)
                        else:
                            ev_fold[(blk_i, pp)] = S.add(
                                "ve", lambda e, p_=pp:
                                e.tensor_tensor(out=SF32[:], in0=SF32[:],
                                                in1=PBUF2[p_ % 2][:, 512:1024],
                                                op=ALU.add),
                                deps=[ev_exp[(blk_i, pp)], ev_fold[(blk_i, pp - 1)]],
                                sig=True)
                ev_zmm = S.add("te", lambda e:
                               e.matmul(out=PSB7[0:1, :], lhsT=ONESFC[:], rhs=SF32[:],
                                        start=False, stop=True),
                               deps=[ev_fold[(blk_i, 7)]], sig=True)
                ev_zmm_d[blk_i] = ev_zmm
                ev_lnz = S.add("se", lambda e:
                               e.activation(out=ZROW[:], in_=PSB7[0:1, :], func=AF.Ln),
                               deps=[ev_zmm], sig=True)
                ev_zi = S.add("se", lambda e:
                              e.activation(out=ZINV[:], in_=ZROW[:],
                                           func=AF.Exp, scale=-1.0),
                              deps=[ev_lnz, ev_zbmm.get(blk_i - 1)], sig=True)
                ev_zbmm[blk_i] = S.add("te", lambda e:
                                       e.matmul(out=PSB7[:, :], lhsT=ONESF[:],
                                                rhs=ZINV[:], start=True, stop=True),
                                       deps=[ev_zi], sig=True)
                ev_zbe[blk_i] = S.add("se", lambda e:
                                      e.activation(out=ZB[:], in_=PSB7[:, :],
                                                   func=AF.Identity),
                                      deps=[ev_zbmm[blk_i], ev_att.get(blk_i - 1)],
                                      sig=True)
                ev_att[blk_i] = S.add(
                    "ve", lambda e, h_=h, e2=e_seg:
                    e.tensor_tensor(out=ATT[:, BLK[(h_, e2)] * 512:(BLK[(h_, e2)] + 1) * 512],
                                    in0=bank(4), in1=ZB[:], op=ALU.mult),
                    deps=[ev_zbe[blk_i]], sig=True)
                blk_i += 1
                nper = {1: 1, 2: 2, 3: 4}.get(e_seg, 0)
                for _ in range(nper):
                    if d_next < e_seg * 4:
                        emit_dtile(d_next, ev_att_done[e_seg - 1])
                        d_next += 1
            ev_att_done[qb] = ev_att[blk_i - 1]
        while d_next < NTC:
            emit_dtile(d_next, ev_att_done[3] if d_next >= 12 else ev_att_done[d_next // 4])
            d_next += 1
        S.barrier()
        tap(f"att_{s}", ATT[:, 0:PACKW])

    od_totals = {k: S.cnt[k] for k in ("OD0", "OD1", "OD2") if S.cnt[k] > 0}
    run = S.runner(sems)

    with es, nc.Block() as blk:
        @blk.tensor
        def _(t):
            run("te", t)

        @blk.vector
        def _(v):
            run("ve", v)

        @blk.scalar
        def _(sc):
            run("se", sc)

        @blk.gpsimd
        def _(g):
            run("ge", g)

        @blk.sync
        def _(sy):
            run("sy", sy)
            for k, v in od_totals.items():
                sy.wait_ge(sems[k], v)

    return nc, taps


def _host_prep(inputs):
    gam = _gammas()
    x = np.ascontiguousarray(np.asarray(inputs["L0_attn_x"], np.float32))
    wts = {
        "wq": inputs["L0_attn_Wq"], "wk": inputs["L0_attn_Wk"],
        "wv": inputs["L0_attn_Wv"], "wo": inputs["L0_attn_Wo"],
        "fq": inputs["L0_ffn_q_w"], "fk": inputs["L0_ffn_k_w"],
        "fv": inputs["L0_ffn_v_w"], "fo": inputs["L0_ffn_o_w"],
    }
    wt_t = {k: np.ascontiguousarray(np.asarray(v, np.float32).T) for k, v in wts.items()}

    def fm(v):
        return np.asarray(v, np.float32).reshape(8, 128).T
    bp = np.ascontiguousarray(np.concatenate(
        [fm(inputs["L0_ffn_q_b"]), fm(inputs["L0_ffn_k_b"]),
         fm(inputs["L0_ffn_norm_b"]), fm(inputs["L0_ffn_norm_w"]),
         fm(inputs["L0_attn_gn_weight"]), fm(inputs["L0_attn_gn_bias"]),
         np.full((128, 1), -CEXP, np.float32)], axis=1))
    ob = np.ascontiguousarray(np.asarray(inputs["L0_ffn_o_b"], np.float32).reshape(1, D))
    vb = np.ascontiguousarray(np.asarray(inputs["L0_ffn_v_b"], np.float32).reshape(1, D))
    ident = np.eye(128, dtype=bfd)
    ii = np.arange(128, dtype=np.float64)
    ghq = np.zeros((128, 1024), np.float64)
    maskc = np.zeros((128, 1024), np.float64)
    kdec = np.zeros((128, 1024), np.float64)
    for h in range(8):
        g = gam[h]
        ghq[:, h * 128:(h + 1) * 128] = (g ** (ii + 1.0))[None, :]
        jj = ii[:, None]
        io = ii[None, :]
        maskc[:, h * 128:(h + 1) * 128] = np.where(io >= jj, g ** (-jj - 1.0), 0.0)
        kdec[:, h * 128:(h + 1) * 128] = (g ** (127.0 - ii))[:, None]
    common = dict(bpack=bp, ob_row=ob, vb_row=vb, ident=ident,
                  ones_bc=np.ones((128, 1), dtype=bfd),
                  ones_rb=np.ones((1, 128), dtype=bfd),
                  ones_rf=np.ones((1, 128), np.float32),
                  ones_cf=np.ones((128, 1), np.float32),
                  ghq=ghq.astype(bfd), maskc=maskc.astype(bfd), kdec=kdec.astype(bfd))
    common.update(wt_t)
    maps = []
    for core in range(NCORES):
        m = dict(common)
        m["x"] = np.ascontiguousarray(x[core * NSEQ:(core + 1) * NSEQ])
        maps.append(m)
    return maps


def kernel(**inputs):
    maps = _host_prep(inputs)
    nc, _ = build()
    res = run_bass_kernel_spmd(nc, maps, core_ids=list(range(NCORES)))
    out = np.concatenate([np.asarray(res.results[i]["out"]) for i in range(NCORES)], axis=0)
    return out.astype(np.float32)


# version 36 (final)
# speedup vs baseline: 1.0148x; 1.0148x over previous
import math
import contextlib
import numpy as np
import ml_dtypes
import concourse.bass as bass
import concourse.mybir as mybir
from concourse.bass_utils import run_bass_kernel_spmd

F32 = mybir.dt.float32
BF16 = mybir.dt.bfloat16
AF = mybir.ActivationFunctionType
ALU = mybir.AluOpType
AX = mybir.AxisListType
bfd = ml_dtypes.bfloat16

NCORES = 8
NSEQ = 2
L = 2048
D = 1024
NTC = 16
EPS = 1e-5
CEXP = 4.0
SM_SCALE = 1.0 / 32.0

LIVE = [(h, e) for e in range(4) for h in range(8 >> e)]
BLK = {he: i for i, he in enumerate(LIVE)}
PACKW = len(LIVE) * 512
PFX_E = {1: 3, 2: 2, 4: 1, 8: 0}


def _gammas():
    lin = np.linspace(np.float32(math.log(1.0 / 32.0)),
                      np.float32(math.log(1.0 / 512.0)), 8, dtype=np.float32)
    return (np.float32(1.0) - np.exp(lin).astype(np.float32)).astype(np.float64)


ENGINES = ("te", "ve", "se", "sy", "ge")
DMA_ROT = {"XD": ["XD0", "XD1", "XD2"], "OD": ["OD0", "OD1", "OD2"],
           "WD": ["WD0", "WD1"], "CD": ["CD0"]}
SEMKEYS = ("te", "ve", "se", "sy", "ge",
           "XD0", "XD1", "XD2", "OD0", "OD1", "OD2", "WD0", "WD1", "CD0")


class Step:
    __slots__ = ("eng", "fn", "deps", "sig", "dma")

    def __init__(self, eng, fn, deps, sig, dma):
        self.eng, self.fn, self.deps, self.sig, self.dma = eng, fn, deps, sig, dma


class Sched:
    def __init__(self):
        self.steps = []
        self.evs = {}
        self.cnt = {k: 0 for k in SEMKEYS}
        self.last = {e: None for e in ENGINES}
        self.barrier_deps = []
        self.pending_barrier = {e: False for e in ENGINES}
        self.auto = 0
        self.dma_rr = {k: 0 for k in DMA_ROT}
        self.dma_last = {k: None for k in SEMKEYS}

    def _autoname(self):
        self.auto += 1
        return f"_a{self.auto}"

    def add(self, eng, fn, deps=(), sig=None, dma=None):
        assert eng in ENGINES
        deps = [d for d in deps if d is not None]
        if self.pending_barrier[eng]:
            deps = list(self.barrier_deps) + deps
            self.pending_barrier[eng] = False
        if dma is not None:
            stream = dma
            semk = DMA_ROT[stream][self.dma_rr[stream] % len(DMA_ROT[stream])]
            self.dma_rr[stream] += 1
            if self.dma_last[semk] is not None:
                deps = deps + [self.dma_last[semk]]
            self.cnt[semk] += 16
            if sig is None or sig is True:
                sig = self._autoname()
            self.evs[sig] = (semk, self.cnt[semk])
            self.dma_last[semk] = sig
            dma = semk
        elif sig is not None:
            if sig is True:
                sig = self._autoname()
            self.cnt[eng] += 1
            self.evs[sig] = (eng, self.cnt[eng])
        for d in deps:
            assert d in self.evs, f"dep {d} undefined"
        st = Step(eng, fn, deps, sig, dma)
        self.steps.append(st)
        self.last[eng] = st
        return sig

    def barrier(self):
        deps = []
        for e in ENGINES:
            st = self.last[e]
            if st is None:
                continue
            if st.sig is None:
                sig = self._autoname()
                semk = st.dma if st.dma is not None else st.eng
                self.cnt[semk] += 16 if st.dma is not None else 1
                self.evs[sig] = (semk, self.cnt[semk])
                st.sig = sig
            deps.append(st.sig)
        self.barrier_deps = deps
        for e in ENGINES:
            self.pending_barrier[e] = True

    def runner(self, sems):
        import collections
        hwm = {e: collections.defaultdict(int) for e in ENGINES}

        def run(eng_name, eng):
            for st in self.steps:
                if st.eng != eng_name:
                    continue
                for d in st.deps:
                    semk, val = self.evs[d]
                    if hwm[eng_name][semk] >= val:
                        continue
                    eng.wait_ge(sems[semk], val)
                    hwm[eng_name][semk] = val
                inst = st.fn(eng)
                if st.sig is not None:
                    semk, val = self.evs[st.sig]
                    inst.then_inc(sems[semk], 16 if st.dma is not None else 1)
        return run


def build(debug_taps=None):
    debug_taps = debug_taps or {}
    nc = bass.Bass("TRN2", target_bir_lowering=False)
    gam = _gammas()

    x_hbm = nc.dram_tensor("x", [NSEQ, L, D], F32, kind="ExternalInput").ap()
    wt = {}
    for name in ("wq", "wk", "wv", "wo", "fq", "fk", "fv", "fo"):
        wt[name] = nc.dram_tensor(name, [D, D], F32, kind="ExternalInput").ap()
    bp_hbm = nc.dram_tensor("bpack", [128, 49], F32, kind="ExternalInput").ap()
    ob_hbm = nc.dram_tensor("ob_row", [1, D], F32, kind="ExternalInput").ap()
    vb_hbm = nc.dram_tensor("vb_row", [1, D], F32, kind="ExternalInput").ap()
    id_hbm = nc.dram_tensor("ident", [128, 128], BF16, kind="ExternalInput").ap()
    onesb_hbm = nc.dram_tensor("ones_bc", [128, 1], BF16, kind="ExternalInput").ap()
    onesrb_hbm = nc.dram_tensor("ones_rb", [1, 128], BF16, kind="ExternalInput").ap()
    onesf_hbm = nc.dram_tensor("ones_rf", [1, 128], F32, kind="ExternalInput").ap()
    onesfc_hbm = nc.dram_tensor("ones_cf", [128, 1], F32, kind="ExternalInput").ap()
    ghq_hbm = nc.dram_tensor("ghq", [128, 1024], BF16, kind="ExternalInput").ap()
    mask_hbm = nc.dram_tensor("maskc", [128, 1024], BF16, kind="ExternalInput").ap()
    kdec_hbm = nc.dram_tensor("kdec", [128, 1024], BF16, kind="ExternalInput").ap()
    out_hbm = nc.dram_tensor("out", [NSEQ, L, D], F32, kind="ExternalOutput").ap()

    es = contextlib.ExitStack()
    T = lambda *a, **k: es.enter_context(nc.sbuf_tensor(*a, **k))
    SEM = lambda n: es.enter_context(nc.semaphore(n))

    BUF_A = T("buf_a", [128, 16384], BF16)
    BUF_B = T("buf_b", [128, 8192], BF16)
    BUF_D = T("buf_d", [128, 16384], BF16)
    RETK = T("retk", [128, 8192], BF16)
    ATT = RETK
    BUF_E = T("buf_e", [128, 16384], BF16)
    Z_FM = T("z_fm", [128, PACKW], BF16)
    W1 = T("w1", [128, 8192], BF16)
    W2 = T("w2", [128, 8192], BF16)
    XST = [BUF_D[:, 8192 + i * 2048: 8192 + (i + 1) * 2048].bitcast(F32) for i in range(4)]
    XST2 = [BUF_D[:, 0:2048].bitcast(F32), BUF_D[:, 2048:4096].bitcast(F32)]
    TM1D = [BUF_D[:, 4096:5120], BUF_D[:, 5120:6144]]
    TM2D = [BUF_D[:, 6144:7168], BUF_D[:, 7168:8192]]
    OSTG = Z_FM[:, 0:2048].bitcast(F32)
    XSTD = [Z_FM[:, 2048:4096].bitcast(F32), Z_FM[:, 4096:6144].bitcast(F32)]
    SQD = BUF_E[:, 0:1024]
    PBUF2 = [T(f"pbuf2{i}", [128, 1024], BF16) for i in range(2)]
    PBUF = [PBUF2[0][:, 0:512], PBUF2[0][:, 512:1024]]
    KBAR = [PBUF2[1][:, 0:512], PBUF2[1][:, 512:1024]]
    SF32 = T("sf32", [128, 512], F32)
    SQD_GN = SF32[:, :].bitcast(BF16)
    SBF = T("sbf", [128, 512], BF16)
    ZB = T("zb", [128, 512], F32)
    ZROW = T("zrow", [1, 512], F32)
    ZINV = T("zinv", [1, 512], F32)
    ID_SB = T("id_sb", [128, 128], BF16)
    ONESB = T("onesb", [128, 1], BF16)
    ONESRB = T("onesrb", [1, 128], BF16)
    ONESF = T("onesf", [1, 128], F32)
    ONESFC = T("onesfc", [128, 1], F32)
    GHQ = T("ghq_sb", [128, 1024], BF16)
    MASKC = T("mask_sb", [128, 1024], BF16)
    KDEC = T("kdec_sb", [128, 1024], BF16)
    BP = T("bp_sb", [128, 49], F32)
    NBCOL = T("nbcol", [128, 8], BF16)
    OBROW = T("obrow", [1, D], BF16)
    VBROW = T("vbrow", [1, D], F32)
    MU = [T(f"mu{s}", [128, 16], F32) for s in range(NSEQ)]
    RR = [T(f"rr{s}", [128, 16], F32) for s in range(NSEQ)]
    ZS = [T(f"zs{s}", [128, 16], F32) for s in range(NSEQ)]
    NMR = [T(f"nmr{s}", [128, 16], F32) for s in range(NSEQ)]
    NMZ = [T(f"nmz{s}", [128, 16], F32) for s in range(NSEQ)]
    ONEPR = [T(f"onepr{s}", [128, 16], F32) for s in range(NSEQ)]
    TMPA = T("tmpa", [128, 16], F32)
    TMPB = T("tmpb", [128, 16], F32)
    TMPC = T("tmpc", [128, 16], F32)
    YSUM = T("ysum", [128, 16], F32)
    GROW = [T(f"grow{i}", [1, 16], F32) for i in range(6)]
    GSC = T("gsc", [128, 8], F32)
    GBI = T("gbi", [128, 8], F32)
    BQ = T("bq", [128, len(LIVE)], F32)
    BK = T("bk", [128, 32], F32)
    VROWCUR = T("vrowcur", [1, D], BF16)
    GBROW = VROWCUR
    GBIB = T("gbib", [128, 8], BF16)

    PSALL = es.enter_context(nc.psum_tensor("psall", [128, 4096], F32))

    def bank(i, lo=0, w=512):
        return PSALL[:, i * 512 + lo: i * 512 + lo + w]
    PST1 = PSALL[:, 7 * 512: 8 * 512].bitcast(BF16)
    PSB7 = PSALL[:, 7 * 512: 8 * 512]
    PST3 = PSALL[:, 6 * 512: 7 * 512].bitcast(BF16)
    PST2 = PSALL[:, 6 * 512 + 256: 7 * 512].bitcast(BF16)
    PSC = PSALL[:, 6 * 512: 6 * 512 + 256]
    PSZ = PSALL[:, 5 * 512: 6 * 512]

    sems = {k: SEM(k) for k in SEMKEYS}
    S = Sched()

    def vchain(eng, fns, deps=()):
        ev = None
        for i, fn in enumerate(fns):
            ev = S.add(eng, fn, deps=(list(deps) + [ev]) if i == 0 else [ev], sig=True)
        return ev

    taps = {}
    for name, shape in debug_taps.items():
        dt = F32 if (len(shape) > 2 and shape[2] == "f32") else BF16
        taps[name] = nc.dram_tensor("tap_" + name, list(shape[:2]), dt,
                                    kind="ExternalOutput").ap()

    def tap(name, buf_ap):
        if name in taps:
            S.add("sy", lambda e, o=taps[name], i=buf_ap: e.dma_start(out=o, in_=i),
                  dma="OD")

    for dst, src in ((ID_SB, id_hbm), (ONESB, onesb_hbm), (ONESRB, onesrb_hbm),
                     (ONESF, onesf_hbm), (ONESFC, onesfc_hbm),
                     (GHQ, ghq_hbm), (MASKC, mask_hbm), (KDEC, kdec_hbm),
                     (BP, bp_hbm), (VBROW, vb_hbm)):
        S.add("sy", lambda e, d=dst, s=src: e.dma_start(out=d[:], in_=s), dma="CD")
    S.add("ge", lambda e: e.dma_start(out=OBROW[:], in_=ob_hbm), dma="WD")
    S.barrier()
    S.add("ve", lambda e: e.tensor_copy(out=NBCOL[:], in_=BP[:, 16:24]), sig=True)
    S.barrier()

    def load_w(dst_sb, src_name, cols=None, deps=()):
        src_ap = wt[src_name]
        if cols is None:
            src = src_ap.rearrange("(c p) o -> p c o", p=128)
            dst = dst_sb[:, 0:8192].rearrange("p (c o) -> p c o", c=8)
        else:
            lo, wdt = cols
            src = src_ap[:, lo:lo + wdt].rearrange("(c p) o -> p c o", p=128)
            dst = dst_sb[:, 0:8 * wdt].rearrange("p (c o) -> p c o", c=8)
        return S.add("ge", lambda e, d=dst, s=src: e.dma_start(out=d, in_=s),
                     deps=deps, dma="WD")

    for s in range(NSEQ):
        ev_red, ev_sq, ev_chain = {}, {}, {}
        ev_xn_ev, ev_z_ev, ev_xn2, ev_z2, ev_trx, ev_trz2 = {}, {}, {}, {}, {}, {}

        def emit_a1(tc):
            ld = S.add("sy", lambda e, s_=s, t_=tc, b=XST[tc % 4]:
                       e.dma_start(out=b, in_=x_hbm[s_, t_ * 128:(t_ + 1) * 128, :]),
                       deps=[ev_red.get(tc - 4), ev_sq.get(tc - 4),
                             ev_xn2.get(tc - 4), ev_z2.get(tc - 4)], dma="XD")
            ev_red[tc] = S.add("ve", lambda e, b=XST[tc % 4], s_=s, t_=tc:
                               e.tensor_reduce(out=MU[s_][:, t_:t_ + 1], in_=b,
                                               axis=AX.X, op=ALU.add),
                               deps=[ld], sig=True)
            ev_sq[tc] = S.add("se", lambda e, b=XST[tc % 4], s_=s, t_=tc:
                              e.activation(out=SQD, in_=b, func=AF.Square,
                                           accum_out=RR[s_][:, t_:t_ + 1]),
                              deps=[ld, ev_sq.get(tc - 1)], sig=True)

        def emit_chain(grp):
            c0, c1 = grp * 2, grp * 2 + 2
            mu_, sq_ = MU[s][:, c0:c1], RR[s][:, c0:c1]
            ta, tb, tcx = TMPA[:, c0:c1], TMPB[:, c0:c1], TMPC[:, c0:c1]
            r_, zs_ = RR[s][:, c0:c1], ZS[s][:, c0:c1]
            nmr_, nmz_ = NMR[s][:, c0:c1], NMZ[s][:, c0:c1]
            evc = vchain("ve", [
                lambda e, o=mu_: e.tensor_scalar(out=o, in0=o, scalar1=1.0 / 1024.0,
                                                 scalar2=None, op0=ALU.mult),
                lambda e, o=sq_: e.tensor_scalar(out=o, in0=o, scalar1=1.0 / 1024.0,
                                                 scalar2=EPS, op0=ALU.mult, op1=ALU.add),
                lambda e, o=ta, i=mu_: e.tensor_tensor(out=o, in0=i, in1=i, op=ALU.mult),
                lambda e, o=ta, i=sq_: e.tensor_tensor(out=o, in0=i, in1=o, op=ALU.subtract),
            ], deps=[ev_red[c0], ev_red[c0 + 1], ev_sq[c0], ev_sq[c0 + 1]])
            evs = S.add("se", lambda e, i=ta, o=tb:
                        e.activation(out=o, in_=i, func=AF.Sqrt),
                        deps=[evc], sig=True)
            evc2 = vchain("ve", [
                lambda e, o=tcx, i=tb: e.reciprocal(out=o, in_=i),
                lambda e, o=tcx, i=ta: e.scalar_tensor_tensor(out=o, in0=i, scalar=0.5,
                                                              in1=o, op0=ALU.mult, op1=ALU.mult),
                lambda e, o=tb, i=tcx: e.scalar_tensor_tensor(out=o, in0=o, scalar=0.5,
                                                              in1=i, op0=ALU.mult, op1=ALU.add),
                lambda e, o=r_, i=tb: e.reciprocal(out=o, in_=i),
                lambda e, o=tb, i=ta: e.reciprocal(out=o, in_=i),
                lambda e, o=tb: e.tensor_scalar(out=o, in0=o, scalar1=-EPS,
                                                scalar2=1.0 + EPS, op0=ALU.mult, op1=ALU.add),
            ], deps=[evs])
            evs2 = S.add("se", lambda e, i=tb, o=tcx:
                         e.activation(out=o, in_=i, func=AF.Sqrt),
                         deps=[evc2], sig=True)
            ev_chain[grp] = vchain("ve", [
                lambda e, o=ta, i=tcx: e.reciprocal(out=o, in_=i),
                lambda e, o=ta, i=tb: e.scalar_tensor_tensor(out=o, in0=i, scalar=0.5,
                                                             in1=o, op0=ALU.mult, op1=ALU.mult),
                lambda e, o=tcx, i=ta: e.scalar_tensor_tensor(out=o, in0=o, scalar=0.5,
                                                              in1=i, op0=ALU.mult, op1=ALU.add),
                lambda e, o=tcx: e.reciprocal(out=o, in_=o),
                lambda e, o=zs_, a=r_, b=tcx: e.tensor_tensor(out=o, in0=a, in1=b, op=ALU.mult),
                lambda e, o=nmr_, a=mu_, b=r_: e.scalar_tensor_tensor(
                    out=o, in0=a, scalar=-1.0, in1=b, op0=ALU.mult, op1=ALU.mult),
                lambda e, o=nmz_, a=mu_, b=zs_: e.scalar_tensor_tensor(
                    out=o, in0=a, scalar=-1.0, in1=b, op0=ALU.mult, op1=ALU.mult),
            ], deps=[evs2])

        def emit_a2(tc):
            e_seg = tc // 4
            nch = 8 >> e_seg
            pst_x = PST1 if tc % 2 == 0 else PST3
            pst_z = PST3 if tc % 2 == 0 else PST1
            tm1, tm2 = TM1D[tc % 2], TM2D[tc % 2]
            ld = S.add("sy", lambda e, s_=s, t_=tc, b=XST2[tc % 2]:
                       e.dma_start(out=b, in_=x_hbm[s_, t_ * 128:(t_ + 1) * 128, :]),
                       deps=[ev_chain[tc // 2], ev_xn2.get(tc - 2), ev_z2.get(tc - 2)],
                       dma="XD")
            ev_xn2[tc] = S.add("se", lambda e, b=XST[tc % 4], s_=s, t_=tc, o=tm1:
                               e.activation(out=o, in_=b, func=AF.Identity,
                                            scale=RR[s_][:, t_:t_ + 1],
                                            bias=NMR[s_][:, t_:t_ + 1]),
                               deps=[ld, ev_trx.get(tc - 2)], sig=True)
            ev_z2[tc] = S.add("se", lambda e, b=XST[tc % 4], s_=s, t_=tc, o=tm2:
                              e.activation(out=o, in_=b, func=AF.Identity,
                                           scale=ZS[s_][:, t_:t_ + 1],
                                           bias=NMZ[s_][:, t_:t_ + 1]),
                              deps=[ld, ev_trz2.get(tc - 2)], sig=True)
            ev_tr = None
            for ch in range(8):
                ev_tr = S.add("te", lambda e, ch_=ch, pp=pst_x, i=tm1:
                              e.transpose(out=pp[:, ch_ * 128:(ch_ + 1) * 128],
                                          in_=i[:, ch_ * 128:(ch_ + 1) * 128],
                                          identity=ID_SB[:]),
                              deps=[ev_xn2[tc], ev_xn_ev.get(tc - 2),
                                    ev_z_ev.get(tc - 1)] if ch == 0 else [],
                              sig=True if ch == 7 else None)
            ev_trx[tc] = ev_tr
            xn_out = BUF_A[:, :].rearrange("p (c f) -> p c f", c=8)[:, :, tc * 128:(tc + 1) * 128]
            ev_xn_ev[tc] = S.add("ve", lambda e, o=xn_out, pp=pst_x:
                                 e.tensor_copy(out=o,
                                               in_=pp[:, :].rearrange("p (c f) -> p c f", c=8)),
                                 deps=[ev_tr], sig=True)
            ev_tz = None
            for ch in range(nch):
                ev_tz = S.add("te", lambda e, ch_=ch, pp=pst_z, i=tm2:
                              e.transpose(out=pp[:, ch_ * 128:(ch_ + 1) * 128],
                                          in_=i[:, ch_ * 128:(ch_ + 1) * 128],
                                          identity=ID_SB[:]),
                              deps=[ev_z2[tc], ev_z_ev.get(tc - 2),
                                    ev_xn_ev.get(tc - 1)] if ch == 0 else [],
                              sig=True if ch == nch - 1 else None)
            ev_trz2[tc] = ev_tz
            zev = None
            for ch in range(nch):
                off = BLK[(ch, e_seg)] * 512 + (tc % 4) * 128
                zev = S.add("ve", lambda e, ch_=ch, off_=off, pp=pst_z:
                            e.tensor_scalar(out=Z_FM[:, off_:off_ + 128],
                                            in0=pp[:, ch_ * 128:(ch_ + 1) * 128],
                                            scalar1=BP[:, 24 + ch_:25 + ch_], scalar2=None,
                                            op0=ALU.mult),
                            deps=[ev_tz] if ch == 0 else [],
                            sig=True if ch == nch - 1 else None)
            ev_z_ev[tc] = zev
        for u in range(NTC + 2):
            if u < NTC:
                emit_a1(u)
            if u % 2 == 1 and u < NTC:
                emit_chain((u - 1) // 2)
            if u >= 2:
                emit_a2(u - 2)
            for _ in range(26):
                S.add("te", lambda e: e.matmul(out=bank(4, 0, 128), lhsT=ID_SB[:],
                                               rhs=ID_SB[:], start=True, stop=True))


        tap(f"xn_fm_{s}", BUF_A[:, :])
        tap(f"z_fm_{s}", Z_FM[:, :])
        tap(f"zs_{s}", ZS[s][:, :])
        tap(f"rr_{s}", RR[s][:, :])
        tap(f"mu_{s}", MU[s][:, :])

        ev_wq_done = ev_wk_done = ev_wv_done = None
        preload = {}

        def load_qkv(g_):
            olo_ = g_ * 512
            evq_ = load_w(W1, "wq", cols=(olo_, 512), deps=[ev_wq_done, ev_wk_done])
            evk_ = S.add("ge", lambda e, o_=olo_:
                         e.dma_start(out=W1[:, 4096:8192].rearrange("p (c o) -> p c o", c=8),
                                     in_=wt["wk"][:, o_:o_ + 512].rearrange("(c p) o -> p c o", p=128)),
                         deps=[ev_wq_done, ev_wk_done], dma="WD")
            evv_ = load_w(W2, "wv", cols=(olo_, 512), deps=[ev_wv_done])
            return evq_, evk_, evv_

        for g in range(2):
            evq, evk, evv = preload.pop(g) if g in preload else load_qkv(g)
            ev_lastuse = {}
            ev_qev, ev_kev, ev_vev = {}, {}, {}
            for which, w0, ev_w in (("q", 0, evq), ("k", 4096, evk)):
                for hl in range(4):
                    for tpass in range(2):
                        for kc in range(8):
                            for tb in (tpass * 2, tpass * 2 + 1):
                                pb = bank(2 + tb % 2)
                                dep = []
                                if kc == 0 and tb % 2 == 0:
                                    dep = [ev_w]
                                if kc == 0:
                                    dep.append(ev_lastuse.get(tb % 2))
                                    dep.append(ev_xn_ev.get(tb * 4 + 3))
                                ev_mm = S.add(
                                    "te", lambda e, w0_=w0, hl_=hl, kc_=kc, tb_=tb, pb=pb:
                                    e.matmul(out=pb,
                                             lhsT=W1[:, w0_ + kc_ * 512 + hl_ * 128:
                                                     w0_ + kc_ * 512 + (hl_ + 1) * 128],
                                             rhs=BUF_A[:, kc_ * 2048 + tb_ * 512:
                                                       kc_ * 2048 + (tb_ + 1) * 512],
                                             start=(kc_ == 0), stop=(kc_ == 7)),
                                    deps=dep, sig=True if kc == 7 else None)
                                if kc == 7:
                                    if which == "q":
                                        h = g * 4 + hl
                                        ghq_rep = GHQ[:, h * 128:(h + 1) * 128] \
                                            .unsqueeze(1).broadcast_to([128, 4, 128])
                                        ev = S.add(
                                            "ve", lambda e, hl_=hl, tb_=tb, gr=ghq_rep, pb=pb:
                                            e.tensor_tensor(
                                                out=BUF_B[:, hl_ * 2048 + tb_ * 512:
                                                          hl_ * 2048 + (tb_ + 1) * 512]
                                                .rearrange("p (c f) -> p c f", c=4),
                                                in0=pb.rearrange("p (c f) -> p c f", c=4),
                                                in1=gr, op=ALU.mult),
                                            deps=[ev_mm], sig=True)
                                        ev_qev[(hl, tb)] = ev
                                    else:
                                        ev = S.add(
                                            "se", lambda e, hl_=hl, tb_=tb, pb=pb:
                                            e.activation(out=RETK[:, hl_ * 2048 + tb_ * 512:
                                                                 hl_ * 2048 + (tb_ + 1) * 512],
                                                         in_=pb, func=AF.Identity),
                                            deps=[ev_mm], sig=True)
                                        ev_kev[(hl, tb)] = ev
                                    ev_lastuse[tb % 2] = ev
            if g == 0:
                ev_wq_done = ev_kev[(3, 3)]
                ev_wk_done = ev_wq_done
            for tc in range(NTC):
                pb = bank(2 + tc % 2)
                for kc in range(8):
                    dep = []
                    if kc == 0:
                        dep = [evv, ev_lastuse.get(tc % 2), ev_xn_ev.get(tc)]
                    ev_mm = S.add("te", lambda e, kc_=kc, tc_=tc, pb=pb:
                                  e.matmul(out=pb,
                                           lhsT=BUF_A[:, kc_ * 2048 + tc_ * 128:
                                                     kc_ * 2048 + (tc_ + 1) * 128],
                                           rhs=W2[:, kc_ * 512:(kc_ + 1) * 512],
                                           start=(kc_ == 0), stop=(kc_ == 7)),
                                  deps=dep, sig=True if kc == 7 else None)
                ev = S.add("se", lambda e, tc_=tc, pb=pb:
                           e.activation(out=BUF_D[:, tc_ * 512:(tc_ + 1) * 512],
                                        in_=pb, func=AF.Identity),
                           deps=[ev_mm], sig=True)
                ev_vev[tc] = ev
                ev_lastuse[tc % 2] = ev
            if g == 0:
                ev_wv_done = ev_vev[15]
            ev_bprojlast = ev_mm
            if g == 0:
                preload[1] = load_qkv(1)

            ev_sinit = S.add("ve", lambda e: e.memset(SF32[:], 0.0), sig=True)
            ev_sup = {-1: S.add("se", lambda e: e.copy(out=SBF[:], in_=SF32[:]),
                                deps=[ev_sinit], sig=True)}
            ev_mask_prev = None
            ev_kbar, ev_ds, ev_ybd = {}, {}, {}
            ev_y_ev = None
            for c in range(NTC):
                ev_sc = None
                prj_deps = [ev_qev.get((hl2, c // 4)) for hl2 in range(4)] + \
                           [ev_kev.get((hl2, c // 4)) for hl2 in range(4)]
                for hl in range(4):
                    ev_sc = S.add("te", lambda e, hl_=hl, c_=c:
                                  e.matmul(out=bank(1, hl_ * 128, 128),
                                           lhsT=RETK[:, hl_ * 2048 + c_ * 128:
                                                    hl_ * 2048 + (c_ + 1) * 128],
                                           rhs=BUF_B[:, hl_ * 2048 + c_ * 128:
                                                     hl_ * 2048 + (c_ + 1) * 128],
                                           start=(hl_ == 0), stop=(hl_ == 3)),
                                  deps=([ev_mask_prev] + prj_deps) if hl == 0 else [],
                                  sig=True if hl == 3 else None)
                mrep = MASKC[:, g * 512:(g + 1) * 512]
                ev_mask = S.add("ve", lambda e, c_=c, mr=mrep:
                                e.tensor_tensor(out=PBUF[c_ % 2], in0=bank(1),
                                                in1=mr, op=ALU.mult),
                                deps=[ev_sc, ev_ybd.get(c - 2)], sig=True)
                ev_mask_prev = ev_mask
                pst = PST1[:, 0:512] if c % 2 == 0 else PST2
                ev_ktr = None
                for hl in range(4):
                    tr_dep = [ev_kbar.get(c - 2)]
                    if c < 2 and g == 0:
                        tr_dep += [ev_z_ev.get(NTC - 1), ev_xn_ev.get(NTC - 1)]
                    ev_ktr = S.add("te", lambda e, hl_=hl, c_=c, p=pst:
                                   e.transpose(out=p[:, hl_ * 128:(hl_ + 1) * 128],
                                               in_=RETK[:, hl_ * 2048 + c_ * 128:
                                                       hl_ * 2048 + (c_ + 1) * 128],
                                               identity=ID_SB[:]),
                                   deps=tr_dep if hl == 0 else [],
                                   sig=True if hl == 3 else None)
                kd = KDEC[:, g * 512:(g + 1) * 512]
                ev_kbar[c] = S.add("ve", lambda e, c_=c, p=pst, kd_=kd:
                                   e.tensor_tensor(out=KBAR[c_ % 2], in0=p,
                                                   in1=kd_, op=ALU.mult),
                                   deps=[ev_ktr, ev_ds.get(c - 2)], sig=True)
                if c > 0:
                    for _ in range(8):
                        S.add("te", lambda e: e.matmul(out=bank(4, 0, 128), lhsT=ID_SB[:],
                                                       rhs=ID_SB[:], start=True, stop=True))
                    for hl in range(4):
                        S.add("te", lambda e, hl_=hl, c_=c:
                              e.matmul(out=bank(0, hl_ * 128, 128),
                                       lhsT=SBF[:, hl_ * 128:(hl_ + 1) * 128],
                                       rhs=BUF_B[:, hl_ * 2048 + c_ * 128:
                                                 hl_ * 2048 + (c_ + 1) * 128],
                                       start=(hl_ == 0), stop=False),
                              deps=[ev_sup[c - 1], ev_y_ev] if hl == 0 else [])
                ev_yb = None
                for hl in range(4):
                    st = (c == 0 and hl == 0)
                    ev_yb = S.add("te", lambda e, hl_=hl, c_=c, st_=st:
                                  e.matmul(out=bank(0, hl_ * 128, 128),
                                           lhsT=BUF_D[:, c_ * 512 + hl_ * 128:
                                                     c_ * 512 + (hl_ + 1) * 128],
                                           rhs=PBUF[c_ % 2][:, hl_ * 128:(hl_ + 1) * 128],
                                           start=st_, stop=(hl_ == 3)),
                                  deps=[ev_mask, ev_y_ev, ev_vev.get(c)] if hl == 0 else [],
                                  sig=True if hl == 3 else None)
                ev_ybd[c] = ev_yb
                ev_d = None
                for hl in range(4):
                    ev_d = S.add("te", lambda e, hl_=hl, c_=c:
                                 e.matmul(out=PSZ[:, hl_ * 128:(hl_ + 1) * 128],
                                          lhsT=KBAR[c_ % 2][:, hl_ * 128:(hl_ + 1) * 128],
                                          rhs=BUF_D[:, c_ * 512 + hl_ * 128:
                                                    c_ * 512 + (hl_ + 1) * 128],
                                          start=(hl_ == 0), stop=(hl_ == 3)),
                                 deps=[ev_kbar[c], ev_ds.get(c - 1)] if hl == 0 else [],
                                 sig=True if hl == 3 else None)

                def supd(e, g_=g):
                    r = None
                    for hl in range(4):
                        gc = float(gam[g_ * 4 + hl] ** 128)
                        r = e.scalar_tensor_tensor(out=SF32[:, hl * 128:(hl + 1) * 128],
                                                   in0=SF32[:, hl * 128:(hl + 1) * 128],
                                                   scalar=gc,
                                                   in1=PSZ[:, hl * 128:(hl + 1) * 128],
                                                   op0=ALU.mult, op1=ALU.add)
                    return r
                ev_sf = S.add("ve", supd, deps=[ev_d, ev_sup[c - 1]], sig=True)
                ev_ds[c] = ev_sf
                ev_sup[c] = S.add("se", lambda e: e.copy(out=SBF[:], in_=SF32[:]),
                                  deps=[ev_sf], sig=True)
                y_out = BUF_E[:, :].rearrange("p (c f) -> p c f", c=8)[
                    :, g * 4:g * 4 + 4, c * 128:(c + 1) * 128]
                ev_y_ev = S.add("se", lambda e, yo=y_out:
                                e.activation(out=yo,
                                             in_=bank(0).rearrange("p (c f) -> p c f", c=4),
                                             func=AF.Identity),
                                deps=[ev_yb], sig=True)
            if g == 1:
                ev_rec_end = S.last["te"].sig
                assert ev_rec_end is not None
                ev_yev_last = ev_y_ev
                ev_mask_last = ev_mask_prev
                ev_supd_last = ev_ds[NTC - 1]
            else:
                S.barrier()

        tap(f"y_fm_{s}", BUF_E[:, :])

        ev_kw = load_w(W2, "fk", deps=[ev_bprojlast])
        ev_qw = load_w(W1, "fq", deps=[ev_bprojlast])
        PSCB = PSALL[:, 4 * 512: 4 * 512 + 256]
        ev_bfree = {}
        ev_beff_prev = None
        ev_bq = {}
        ev_qfe, ev_kevF = {}, {}
        for h in range(8):
            bcnt = 0
            nbeff = 15
            for e_ in range(4):
                nch = 8 >> e_
                pb = bank(2 + e_ % 2)
                ev_mm = None
                for kc in range(nch):
                    dep = []
                    if kc == 0:
                        dep = [ev_kw, ev_bfree.get(2 + e_ % 2), ev_z_ev.get(NTC - 1)]
                    ev_mm = S.add(
                        "te", lambda e, h_=h, kc_=kc, e2=e_, nch_=nch, pb=pb:
                        e.matmul(out=pb,
                                 lhsT=W2[:, kc_ * 1024 + h_ * 128:kc_ * 1024 + (h_ + 1) * 128],
                                 rhs=Z_FM[:, BLK[(kc_, e2)] * 512:(BLK[(kc_, e2)] + 1) * 512],
                                 start=(kc_ == 0), stop=(kc_ == nch_ - 1)),
                        deps=dep, sig=True if kc == nch - 1 else None)
                    if s == 0:
                        bcnt += 1
                        S.add("te", lambda e, h_=h, kc_=kc, e2=e_, st_=(bcnt == 1),
                              sp_=(bcnt == nbeff):
                              e.matmul(out=PSCB[:, e2:e2 + 1],
                                       lhsT=W2[:, kc_ * 1024 + h_ * 128:
                                               kc_ * 1024 + (h_ + 1) * 128],
                                       rhs=NBCOL[:, kc_:kc_ + 1],
                                       start=st_, stop=sp_),
                              deps=[ev_beff_prev] if bcnt == 1 else [],
                              sig=True if bcnt == nbeff else None)
                        if bcnt == nbeff:
                            ev_blast = S.steps[-1].sig
                ev_bfree[("kmm", h, e_)] = ev_mm
                if s != 0:
                    ev = S.add("se", lambda e, h_=h, e2=e_, pb=pb:
                               e.activation(out=BUF_A[:, h_ * 2048 + e2 * 512:
                                                    h_ * 2048 + (e2 + 1) * 512],
                                            in_=pb, func=AF.Identity,
                                            bias=BK[:, h_ * 4 + e2:h_ * 4 + e2 + 1]),
                               deps=[ev_mm, ev_bprojlast], sig=True)
                    ev_bfree[2 + e_ % 2] = ev
                    ev_kevF[h] = ev
            if s == 0:
                evl = None
                for e_ in range(4):
                    evl = S.add("ve", lambda e, h_=h, e2=e_:
                                e.tensor_scalar(out=BK[:, h_ * 4 + e2:h_ * 4 + e2 + 1],
                                                in0=PSCB[:, e2:e2 + 1],
                                                scalar1=BP[:, 8 + h_:9 + h_], scalar2=None,
                                                op0=ALU.add),
                                deps=[ev_blast], sig=True)
                    ev_bq[("k", h, e_)] = evl
                ev_beff_prev = evl
                for e_ in range(4):
                    ev = S.add("se", lambda e, h_=h, e2=e_, pb=bank(2 + e_ % 2):
                               e.activation(out=BUF_A[:, h_ * 2048 + e2 * 512:
                                                    h_ * 2048 + (e2 + 1) * 512],
                                            in_=pb, func=AF.Identity,
                                            bias=BK[:, h_ * 4 + e2:h_ * 4 + e2 + 1]),
                               deps=[ev_bfree[("kmm", h, e_)], ev_bq[("k", h, e_)],
                                     ev_bprojlast], sig=True)
                    ev_bfree[2 + e_ % 2] = ev
                    ev_kevF[h] = ev
        ev_kdone = S.last["te"].sig
        assert ev_kdone is not None
        ev_vw0 = S.add("ge", lambda e:
                       e.dma_start(out=RETK[:, 0:4096].rearrange("p (c o) -> p c o", c=4),
                                   in_=wt["fv"][0:512, :].rearrange("(c p) o -> p c o", p=128)),
                       deps=[ev_rec_end], dma="WD")
        ev_vw = S.add("ge", lambda e:
                      e.dma_start(out=RETK[:, 4096:8192].rearrange("p (c o) -> p c o", c=4),
                                  in_=wt["fv"][512:1024, :].rearrange("(c p) o -> p c o", p=128)),
                      deps=[ev_rec_end], dma="WD")
        ev_ow = load_w(W2, "fo", deps=[ev_kdone])
        ev_beff_done = ev_beff_prev if s == 0 else None
        for h in range(8):
            lives = [e_ for e_ in range(4) if h < (8 >> e_)]
            maxk = 8 >> min(lives) if lives else 0
            nbeff = sum(1 for e_ in lives for kc in range(8 >> e_))
            bcnt = 0
            for kc in range(maxk):
                for e_ in lives:
                    nch = 8 >> e_
                    if kc >= nch:
                        continue
                    dep = []
                    if kc == 0:
                        dep = [ev_qw, ev_bfree.get(e_)]
                        if h == 0 and e_ < 2:
                            dep += [ev_yev_last, ev_mask_last]
                    ev_mm = S.add(
                        "te", lambda e, h_=h, kc_=kc, e2=e_, nch_=nch:
                        e.matmul(out=bank(e2),
                                 lhsT=W1[:, kc_ * 1024 + h_ * 128:kc_ * 1024 + (h_ + 1) * 128],
                                 rhs=Z_FM[:, BLK[(kc_, e2)] * 512:(BLK[(kc_, e2)] + 1) * 512],
                                 start=(kc_ == 0), stop=(kc_ == nch_ - 1)),
                        deps=dep, sig=True if kc == nch - 1 else None)
                    if s == 0:
                        bcnt += 1
                        S.add("te", lambda e, h_=h, kc_=kc, e2=e_, st_=(bcnt == 1),
                              sp_=(bcnt == nbeff):
                              e.matmul(out=PSCB[:, e2:e2 + 1],
                                       lhsT=W1[:, kc_ * 1024 + h_ * 128:
                                               kc_ * 1024 + (h_ + 1) * 128],
                                       rhs=NBCOL[:, kc_:kc_ + 1],
                                       start=st_, stop=sp_),
                              deps=[ev_beff_prev] if bcnt == 1 else [],
                              sig=True if bcnt == nbeff else None)
                        if bcnt == nbeff:
                            ev_blast = S.steps[-1].sig
                    if kc == nch - 1 and s != 0:
                        ev = S.add("se", lambda e, h_=h, e2=e_:
                                   e.activation(out=BUF_B[:, BLK[(h_, e2)] * 512:
                                                        (BLK[(h_, e2)] + 1) * 512],
                                                in_=bank(e2), func=AF.Identity,
                                                bias=BQ[:, BLK[(h_, e2)]:BLK[(h_, e2)] + 1]),
                                   deps=[ev_mm], sig=True)
                        ev_bfree[e_] = ev
                        ev_qfe[(h, e_)] = ev
                    elif kc == nch - 1:
                        ev_bfree[(h, e_, "mm")] = ev_mm
            if s == 0:
                evl = None
                for e_ in lives:
                    ev_bq[(h, e_)] = S.add(
                        "ve", lambda e, h_=h, e2=e_:
                        e.tensor_scalar(out=BQ[:, BLK[(h_, e2)]:BLK[(h_, e2)] + 1],
                                        in0=PSCB[:, e2:e2 + 1],
                                        scalar1=BP[:, h_:h_ + 1], scalar2=None,
                                        op0=ALU.add),
                        deps=[ev_blast], sig=True)
                    evl = ev_bq[(h, e_)]
                ev_beff_prev = evl
                for e_ in lives:
                    ev = S.add("se", lambda e, h_=h, e2=e_:
                               e.activation(out=BUF_B[:, BLK[(h_, e2)] * 512:
                                                    (BLK[(h_, e2)] + 1) * 512],
                                            in_=bank(e2), func=AF.Identity,
                                            bias=BQ[:, BLK[(h_, e2)]:BLK[(h_, e2)] + 1]),
                               deps=[ev_bfree[(h, e_, "mm")], ev_bq[(h, e_)]], sig=True)
                    ev_bfree[e_] = ev
                    ev_qfe[(h, e_)] = ev
        ev_qdone = S.last["te"].sig
        assert ev_qdone is not None
        ev_wo = load_w(W1, "wo", deps=[ev_qdone])
        ev_r = None
        ev_q = None
        for ch in range(8):
            ev_r = S.add("ve", lambda e, ch_=ch:
                         e.tensor_reduce(out=YSUM[:, ch_:ch_ + 1],
                                         in_=BUF_E[:, ch_ * 2048:(ch_ + 1) * 2048],
                                         axis=AX.X, op=ALU.add),
                         deps=[ev_yev_last] if ch == 0 else [], sig=True)
            for half in range(2):
                dst = TMPA if half == 0 else TMPB
                ev_q = S.add("se", lambda e, ch_=ch, h_=half, d=dst:
                             e.activation(out=SQD_GN,
                                          in_=BUF_E[:, ch_ * 2048 + h_ * 1024:
                                                   ch_ * 2048 + (h_ + 1) * 1024],
                                          func=AF.Square, accum_out=d[:, ch_:ch_ + 1]),
                             deps=[ev_q], sig=True)
        ev_sq2 = S.add("ve", lambda e:
                       e.tensor_tensor(out=YSUM[:, 8:16], in0=TMPA[:, 0:8],
                                       in1=TMPB[:, 0:8], op=ALU.add),
                       deps=[ev_q, ev_r], sig=True)
        n_ = 1.0 / (2048.0 * 128.0)
        r0, r1, r2_, r3, r4, r5 = (GROW[i][0:1, :] for i in range(6))
        gnev = {}

        def emit_gn_a():
            ev_gn0 = S.add("te", lambda e:
                           e.matmul(out=PSC[0:1, 0:16], lhsT=ONESFC[:], rhs=YSUM[:],
                                    start=True, stop=True),
                           deps=[ev_sq2], sig=True)
            ev_gn1 = vchain("ve", [
                lambda e: e.tensor_scalar(out=r0, in0=PSC[0:1, 0:16], scalar1=n_,
                                          scalar2=None, op0=ALU.mult),
                lambda e: e.tensor_tensor(out=r1[:, 0:8], in0=r0[0:1, 0:8],
                                          in1=r0[0:1, 0:8], op=ALU.mult),
                lambda e: e.tensor_scalar(out=r2_[:, 0:8], in0=r0[0:1, 8:16],
                                          scalar1=EPS, scalar2=None, op0=ALU.add),
                lambda e: e.tensor_tensor(out=r1[:, 0:8], in0=r2_[:, 0:8],
                                          in1=r1[:, 0:8], op=ALU.subtract),
            ], deps=[ev_gn0])
            ev_gn2 = S.add("se", lambda e:
                           e.activation(out=GROW[2][0:1, 0:8], in_=GROW[1][0:1, 0:8],
                                        func=AF.Sqrt),
                           deps=[ev_gn1], sig=True)
            gnev["gn3"] = vchain("ve", [
                lambda e: e.reciprocal(out=r3[:, 0:8], in_=r2_[:, 0:8]),
                lambda e: e.scalar_tensor_tensor(out=r3[:, 0:8], in0=r1[:, 0:8], scalar=0.5,
                                                 in1=r3[:, 0:8], op0=ALU.mult, op1=ALU.mult),
                lambda e: e.scalar_tensor_tensor(out=r2_[:, 0:8], in0=r2_[:, 0:8], scalar=0.5,
                                                 in1=r3[:, 0:8], op0=ALU.mult, op1=ALU.add),
                lambda e: e.reciprocal(out=r4[:, 0:8], in_=r2_[:, 0:8]),
                lambda e: e.scalar_tensor_tensor(out=r5[:, 0:8], in0=r0[0:1, 0:8], scalar=-1.0,
                                                 in1=r4[:, 0:8], op0=ALU.mult, op1=ALU.mult),
            ], deps=[ev_gn2])

        def emit_gn_b():
            ev_gn4 = S.add("te", lambda e:
                           e.matmul(out=PSC[:, 32:40], lhsT=ONESF[:], rhs=GROW[4][0:1, 0:8],
                                    start=True, stop=True),
                           deps=[gnev["gn3"]], sig=True)
            ev_gn5 = S.add("te", lambda e:
                           e.matmul(out=PSC[:, 40:48], lhsT=ONESF[:], rhs=GROW[5][0:1, 0:8],
                                    start=True, stop=True),
                           sig=True)
            gnev["cols"] = vchain("ve", [
                lambda e: e.tensor_tensor(out=GSC[:, 0:8], in0=PSC[:, 32:40],
                                          in1=BP[:, 32:40], op=ALU.mult),
                lambda e: e.tensor_tensor(out=GBI[:, 0:8], in0=PSC[:, 40:48],
                                          in1=BP[:, 32:40], op=ALU.mult),
                lambda e: e.tensor_tensor(out=GBI[:, 0:8], in0=GBI[:, 0:8],
                                          in1=BP[:, 40:48], op=ALU.add),
            ], deps=[ev_gn4, ev_gn5])
        ev_vfree = {}
        ev_vrow = None
        for tc in range(NTC):
            if tc == 5:
                emit_gn_a()
            if tc == 7:
                emit_gn_b()
            e_seg = tc // 4
            nch = 8 >> e_seg
            pk = (tc % 2) * 2
            if tc % 4 == 0:
                for half in range(2):
                    ev_mm = None
                    for kc in range(nch):
                        ev_mm = S.add(
                            "te", lambda e, kc_=kc, h_=half, nch_=nch:
                            e.matmul(out=PSZ[0:1, 0:512],
                                     lhsT=NBCOL[:, kc_:kc_ + 1],
                                     rhs=RETK[:, kc_ * 1024 + h_ * 512:
                                              kc_ * 1024 + (h_ + 1) * 512],
                                     start=(kc_ == 0), stop=(kc_ == nch_ - 1)),
                            deps=[ev_vw0, ev_vw, ev_vrow, ev_supd_last] if kc == 0 else [],
                            sig=True if kc == nch - 1 else None)
                    ev_vrow = S.add("ve", lambda e, h_=half:
                                    e.tensor_tensor(out=VROWCUR[0:1, h_ * 512:(h_ + 1) * 512],
                                                    in0=PSZ[0:1, 0:512],
                                                    in1=VBROW[0:1, h_ * 512:(h_ + 1) * 512],
                                                    op=ALU.add),
                                    deps=[ev_mm], sig=True)
            for kc in range(nch):
                for half in range(2):
                    S.add("te", lambda e, kc_=kc, tc_=tc, h_=half, pk_=pk, e2=e_seg:
                          e.matmul(out=bank(pk_ + h_),
                                   lhsT=Z_FM[:, BLK[(kc_, e2)] * 512 + (tc_ % 4) * 128:
                                             BLK[(kc_, e2)] * 512 + (tc_ % 4 + 1) * 128],
                                   rhs=RETK[:, kc_ * 1024 + h_ * 512:kc_ * 1024 + (h_ + 1) * 512],
                                   start=(kc_ == 0), stop=False),
                          deps=[ev_vw0, ev_vw, ev_vfree.get(tc - 2),
                                ev_bfree.get((tc % 2) * 2), ev_bfree.get((tc % 2) * 2 + 1),
                                ev_rec_end if tc < 2 else None]
                          if (kc == 0 and half == 0) else [])
            ev_mm = None
            for half in range(2):
                ev_mm = S.add("te", lambda e, tc_=tc, h_=half, pk_=pk:
                              e.matmul(out=bank(pk_ + h_),
                                       lhsT=ONESRB[:],
                                       rhs=VROWCUR[0:1, h_ * 512:(h_ + 1) * 512],
                                       start=False, stop=True),
                              deps=[ev_vrow] if half == 0 else [],
                              sig=True if half == 1 else None)
            ev_vfree[tc] = S.add("ve", lambda e, tc_=tc, pk_=pk:
                                 e.tensor_copy(out=BUF_D[:, tc_ * 1024:(tc_ + 1) * 1024],
                                               in_=bank(pk_, 0, 1024)),
                                 deps=[ev_mm], sig=True)
        tap(f"qf_{s}", BUF_B[:, 0:PACKW])
        tap(f"kf_{s}", BUF_A[:, :])
        tap(f"vt_{s}", BUF_D[:, :])

        ev_dprep = S.add("ve", lambda e, s_=s:
                         e.tensor_scalar(out=ONEPR[s_][:], in0=RR[s_][:], scalar1=1.0,
                                         scalar2=None, op0=ALU.add), sig=True)
        ev_gbib = S.add("ve", lambda e: e.tensor_copy(out=GBIB[:], in_=GBI[:]),
                        deps=[gnev["cols"]], sig=True)
        ev_grow = None
        for half in range(2):
            ev_mm = None
            for kc in range(8):
                ev_mm = S.add("te", lambda e, kc_=kc, h_=half:
                              e.matmul(out=PSB7[0:1, :],
                                       lhsT=GBIB[:, kc_:kc_ + 1],
                                       rhs=W1[:, kc_ * 1024 + h_ * 512:
                                               kc_ * 1024 + (h_ + 1) * 512],
                                       start=(kc_ == 0), stop=(kc_ == 7)),
                              deps=[ev_wo, ev_gbib, ev_grow,
                                    ev_vfree.get(NTC - 1)] if kc == 0 else [],
                              sig=True if kc == 7 else None)
            ev_grow = S.add("ve", lambda e, h_=half:
                            e.tensor_copy(out=GBROW[0:1, h_ * 512:(h_ + 1) * 512],
                                          in_=PSB7[0:1, 0:512]),
                            deps=[ev_mm], sig=True)
        ev_w1s = None
        for ch in range(8):
            ev_w1s = S.add("ve", lambda e, ch_=ch:
                           e.tensor_scalar(out=W1[:, ch_ * 1024:(ch_ + 1) * 1024],
                                           in0=W1[:, ch_ * 1024:(ch_ + 1) * 1024],
                                           scalar1=GSC[:, ch_:ch_ + 1], scalar2=None,
                                           op0=ALU.mult),
                           deps=[ev_grow, gnev["cols"]] if ch == 0 else [],
                           sig=True if ch == 7 else None)

        ev_exp, ev_z, ev_fold, ev_att, ev_zbe, ev_zbmm = {}, {}, {}, {}, {}, {}
        ev_zmm_d = {}
        ev_out_tt, ev_store, ev_t0_d = {}, {}, {}
        XDEP2 = {0: 6, 1: 7}
        blk_i = 0

        def emit_dtile(tc, ev_att_qb):
            e_seg = tc // 4
            me = D >> e_seg
            nch = 8 >> e_seg
            nho = (me + 511) // 512
            ld = S.add("sy", lambda e, s_=s, t_=tc, b=XSTD[tc % 2]:
                       e.dma_start(out=b, in_=x_hbm[s_, t_ * 128:(t_ + 1) * 128, :]),
                       deps=[ev_t0_d.get(tc - 2),
                             ev_vfree[NTC - 1] if tc < 2 else None], dma="XD")
            for kc in range(8):
                for half in range(2):
                    dep = []
                    if kc == 0 and half == 0:
                        dep = [ev_w1s, ev_out_tt.get(tc - 1), ev_beff_done]
                    S.add("te", lambda e, kc_=kc, h_=half, tc_=tc:
                          e.matmul(out=bank(5 + h_),
                                   lhsT=BUF_E[:, kc_ * 2048 + tc_ * 128:
                                             kc_ * 2048 + (tc_ + 1) * 128],
                                   rhs=W1[:, kc_ * 1024 + h_ * 512:kc_ * 1024 + (h_ + 1) * 512],
                                   start=(kc_ == 0), stop=False),
                          deps=dep)
            for kc in range(nch):
                for half in range(nho):
                    wdt = min(512, me - half * 512)
                    S.add("te", lambda e, kc_=kc, h_=half, tc_=tc, e2=e_seg, w_=wdt:
                          e.matmul(out=bank(5 + h_, 0, w_),
                                   lhsT=ATT[:, BLK[(kc_, e2)] * 512 + (tc_ % 4) * 128:
                                            BLK[(kc_, e2)] * 512 + (tc_ % 4 + 1) * 128],
                                   rhs=W2[:, kc_ * 1024 + h_ * 512:kc_ * 1024 + h_ * 512 + w_],
                                   start=False, stop=False),
                          deps=[ev_ow, ev_att_qb] if (kc == 0 and half == 0) else [])
            for half in range(2):
                S.add("te", lambda e, h_=half, sp_=(half == 1 and me <= 512):
                      e.matmul(out=bank(5 + h_), lhsT=ONESRB[:],
                               rhs=GBROW[0:1, h_ * 512:(h_ + 1) * 512],
                               start=False, stop=sp_), deps=[])
            ev_last = None
            for half in range(nho):
                wdt = min(512, me - half * 512)
                ev_last = S.add("te", lambda e, h_=half, w_=wdt:
                                e.matmul(out=bank(5 + h_, 0, w_), lhsT=ONESRB[:],
                                         rhs=OBROW[0:1, h_ * 512:h_ * 512 + w_],
                                         start=False, stop=(h_ == 0 or me > 512)),
                                sig=True)
            ev_t0 = S.add("se", lambda e, s_=s, t_=tc, b=XSTD[tc % 2]:
                          e.activation(out=OSTG, in_=b, func=AF.Identity,
                                       scale=ONEPR[s_][:, t_:t_ + 1],
                                       bias=NMR[s_][:, t_:t_ + 1]),
                          deps=[ld, ev_dprep, ev_store.get(tc - 1)], sig=True)
            ev_t0_d[tc] = ev_t0
            ev_out_tt[tc] = S.add("ve", lambda e:
                                  e.tensor_tensor(out=OSTG, in0=bank(5, 0, 1024),
                                                  in1=OSTG, op=ALU.add),
                                  deps=[ev_last, ev_t0], sig=True)
            ev_store[tc] = S.add("sy", lambda e, s_=s, t_=tc:
                                 e.dma_start(out=out_hbm[s_, t_ * 128:(t_ + 1) * 128, :],
                                             in_=OSTG),
                                 deps=[ev_out_tt[tc]], dma="OD")

        ev_att_done = {}
        d_next = 0
        for qb in range(4):
            e_seg = qb
            for h in range(8 >> e_seg):
                qoff = BLK[(h, e_seg)] * 512
                for p in range(9):
                    if p < 8:
                        sb = (p % 2) * 2
                        dep_sc = [ev_exp.get((blk_i, p - 2))]
                        if p < 2:
                            dep_sc += [ev_exp.get((blk_i - 1, XDEP2[p])),
                                       ev_kevF.get(h), ev_qfe.get((h, e_seg))]
                            if blk_i == 0:
                                dep_sc.append(ev_vfree[NTC - 1])
                        ev_sc = None
                        for half in range(2):
                            c = 2 * p + half
                            ev_sc = S.add(
                                "te", lambda e, h_=h, c_=c, q0=qoff, b_=sb + half:
                                e.matmul(out=bank(b_),
                                         lhsT=BUF_A[:, h_ * 2048 + c_ * 128:
                                                  h_ * 2048 + (c_ + 1) * 128],
                                         rhs=BUF_B[:, q0:q0 + 512],
                                         start=True, stop=True),
                                deps=dep_sc if half == 0 else [],
                                sig=True if half == 1 else None)
                        ev_exp[(blk_i, p)] = S.add(
                            "se", lambda e, p_=p, b_=sb:
                            e.activation(out=PBUF2[p_ % 2][:],
                                         in_=PSALL[:, b_ * 512:b_ * 512 + 1024],
                                         func=AF.Exp, scale=SM_SCALE, bias=BP[:, 48:49]),
                            deps=[ev_sc, ev_z.get((blk_i, p - 2)),
                                  ev_fold.get((blk_i, p - 2)),
                                  ev_z.get((blk_i - 1, p + 6)),
                                  ev_fold.get((blk_i - 1, p + 6))], sig=True)
                    if p >= 1:
                        pp = p - 1
                        dep_pv = [ev_exp[(blk_i, pp)], ev_vfree.get(2 * pp + 1)]
                        if pp == 0:
                            dep_pv += [ev_att.get(blk_i - 1)]
                        for half in range(2):
                            c = 2 * pp + half
                            S.add("te", lambda e, h_=h, c_=c, p_=pp, hf=half:
                                  e.matmul(out=bank(4),
                                           lhsT=BUF_D[:, c_ * 1024 + h_ * 128:
                                                     c_ * 1024 + (h_ + 1) * 128],
                                           rhs=PBUF2[p_ % 2][:, hf * 512:(hf + 1) * 512],
                                           start=(c_ == 0), stop=(c_ == 15)),
                                  deps=dep_pv if half == 0 else [])
                        dep_z = []
                        if pp == 0:
                            dep_z = [ev_zbe.get(blk_i - 1)]
                            if blk_i == 0:
                                dep_z.append(ev_grow)
                        ev_z[(blk_i, pp)] = S.add(
                            "te", lambda e, p_=pp:
                            e.matmul(out=PSB7[0:1, :], lhsT=ONESB[:],
                                     rhs=PBUF2[p_ % 2][:, 0:512],
                                     start=(p_ == 0), stop=False),
                            deps=dep_z, sig=True)
                        if pp == 0:
                            ev_fold[(blk_i, pp)] = S.add(
                                "ve", lambda e, p_=pp:
                                e.tensor_scalar(out=SF32[:],
                                                in0=PBUF2[p_ % 2][:, 512:1024],
                                                scalar1=1.0, scalar2=None, op0=ALU.mult),
                                deps=[ev_exp[(blk_i, pp)], ev_zmm_d.get(blk_i - 1),
                                      ev_sq2 if blk_i == 0 else None],
                                sig=True)
                        else:
                            ev_fold[(blk_i, pp)] = S.add(
                                "ve", lambda e, p_=pp:
                                e.tensor_tensor(out=SF32[:], in0=SF32[:],
                                                in1=PBUF2[p_ % 2][:, 512:1024],
                                                op=ALU.add),
                                deps=[ev_exp[(blk_i, pp)], ev_fold[(blk_i, pp - 1)]],
                                sig=True)
                ev_zmm = S.add("te", lambda e:
                               e.matmul(out=PSB7[0:1, :], lhsT=ONESFC[:], rhs=SF32[:],
                                        start=False, stop=True),
                               deps=[ev_fold[(blk_i, 7)]], sig=True)
                ev_zmm_d[blk_i] = ev_zmm
                ev_lnz = S.add("se", lambda e:
                               e.activation(out=ZROW[:], in_=PSB7[0:1, :], func=AF.Ln),
                               deps=[ev_zmm], sig=True)
                ev_zi = S.add("se", lambda e:
                              e.activation(out=ZINV[:], in_=ZROW[:],
                                           func=AF.Exp, scale=-1.0),
                              deps=[ev_lnz, ev_zbmm.get(blk_i - 1)], sig=True)
                ev_zbmm[blk_i] = S.add("te", lambda e:
                                       e.matmul(out=PSB7[:, :], lhsT=ONESF[:],
                                                rhs=ZINV[:], start=True, stop=True),
                                       deps=[ev_zi], sig=True)
                ev_zbe[blk_i] = S.add("se", lambda e:
                                      e.activation(out=ZB[:], in_=PSB7[:, :],
                                                   func=AF.Identity),
                                      deps=[ev_zbmm[blk_i], ev_att.get(blk_i - 1)],
                                      sig=True)
                ev_att[blk_i] = S.add(
                    "ve", lambda e, h_=h, e2=e_seg:
                    e.tensor_tensor(out=ATT[:, BLK[(h_, e2)] * 512:(BLK[(h_, e2)] + 1) * 512],
                                    in0=bank(4), in1=ZB[:], op=ALU.mult),
                    deps=[ev_zbe[blk_i]], sig=True)
                blk_i += 1
                nper = {1: 1, 2: 2, 3: 4}.get(e_seg, 0)
                for _ in range(nper):
                    if d_next < e_seg * 4:
                        emit_dtile(d_next, ev_att_done[e_seg - 1])
                        d_next += 1
            ev_att_done[qb] = ev_att[blk_i - 1]
        while d_next < NTC:
            emit_dtile(d_next, ev_att_done[3] if d_next >= 12 else ev_att_done[d_next // 4])
            d_next += 1
        S.barrier()
        tap(f"att_{s}", ATT[:, 0:PACKW])

    od_totals = {k: S.cnt[k] for k in ("OD0", "OD1", "OD2") if S.cnt[k] > 0}
    run = S.runner(sems)

    with es, nc.Block() as blk:
        @blk.tensor
        def _(t):
            run("te", t)

        @blk.vector
        def _(v):
            run("ve", v)

        @blk.scalar
        def _(sc):
            run("se", sc)

        @blk.gpsimd
        def _(g):
            run("ge", g)

        @blk.sync
        def _(sy):
            run("sy", sy)
            for k, v in od_totals.items():
                sy.wait_ge(sems[k], v)

    return nc, taps


def _host_prep(inputs):
    gam = _gammas()
    x = np.ascontiguousarray(np.asarray(inputs["L0_attn_x"], np.float32))
    wts = {
        "wq": inputs["L0_attn_Wq"], "wk": inputs["L0_attn_Wk"],
        "wv": inputs["L0_attn_Wv"], "wo": inputs["L0_attn_Wo"],
        "fq": inputs["L0_ffn_q_w"], "fk": inputs["L0_ffn_k_w"],
        "fv": inputs["L0_ffn_v_w"], "fo": inputs["L0_ffn_o_w"],
    }
    wt_t = {k: np.ascontiguousarray(np.asarray(v, np.float32).T) for k, v in wts.items()}

    def fm(v):
        return np.asarray(v, np.float32).reshape(8, 128).T
    bp = np.ascontiguousarray(np.concatenate(
        [fm(inputs["L0_ffn_q_b"]), fm(inputs["L0_ffn_k_b"]),
         fm(inputs["L0_ffn_norm_b"]), fm(inputs["L0_ffn_norm_w"]),
         fm(inputs["L0_attn_gn_weight"]), fm(inputs["L0_attn_gn_bias"]),
         np.full((128, 1), -CEXP, np.float32)], axis=1))
    ob = np.ascontiguousarray(np.asarray(inputs["L0_ffn_o_b"], np.float32).reshape(1, D))
    vb = np.ascontiguousarray(np.asarray(inputs["L0_ffn_v_b"], np.float32).reshape(1, D))
    ident = np.eye(128, dtype=bfd)
    ii = np.arange(128, dtype=np.float64)
    ghq = np.zeros((128, 1024), np.float64)
    maskc = np.zeros((128, 1024), np.float64)
    kdec = np.zeros((128, 1024), np.float64)
    for h in range(8):
        g = gam[h]
        ghq[:, h * 128:(h + 1) * 128] = (g ** (ii + 1.0))[None, :]
        jj = ii[:, None]
        io = ii[None, :]
        maskc[:, h * 128:(h + 1) * 128] = np.where(io >= jj, g ** (-jj - 1.0), 0.0)
        kdec[:, h * 128:(h + 1) * 128] = (g ** (127.0 - ii))[:, None]
    common = dict(bpack=bp, ob_row=ob, vb_row=vb, ident=ident,
                  ones_bc=np.ones((128, 1), dtype=bfd),
                  ones_rb=np.ones((1, 128), dtype=bfd),
                  ones_rf=np.ones((1, 128), np.float32),
                  ones_cf=np.ones((128, 1), np.float32),
                  ghq=ghq.astype(bfd), maskc=maskc.astype(bfd), kdec=kdec.astype(bfd))
    common.update(wt_t)
    maps = []
    for core in range(NCORES):
        m = dict(common)
        m["x"] = np.ascontiguousarray(x[core * NSEQ:(core + 1) * NSEQ])
        maps.append(m)
    return maps


def kernel(**inputs):
    maps = _host_prep(inputs)
    nc, _ = build()
    res = run_bass_kernel_spmd(nc, maps, core_ids=list(range(NCORES)))
    out = np.concatenate([np.asarray(res.results[i]["out"]) for i in range(NCORES)], axis=0)
    return out.astype(np.float32)
